# version 06
# baseline (speedup 1.0000x reference)
import numpy as np
import ml_dtypes
import concourse.bass as bass
import concourse.mybir as mybir
from concourse.bass_utils import run_bass_kernel_spmd

F32 = mybir.dt.float32
BF16 = mybir.dt.bfloat16
AF = mybir.ActivationFunctionType
ALU = mybir.AluOpType
AX = mybir.AxisListType

N_CORES = 8
TPC = 4096
ST = 512
NT = 4
D = 2048
NH = 16
EPS = 1e-5



class Buf:
    __slots__ = ("name", "writers", "readers")

    def __init__(self, name):
        self.name = name
        self.writers = {}
        self.readers = {}


class Ins:
    __slots__ = ("waits", "build", "incs")

    def __init__(self, waits, build, incs):
        self.waits, self.build, self.incs = waits, build, incs


class Prog:
    STREAMS = ("T", "V", "S", "Y", "G")

    def __init__(self, nc, sems):
        self.nc = nc
        self.sems = sems
        self.cnt = {k: 0 for k in sems}
        self.ir = {s: [] for s in self.STREAMS}
        self.waited = {s: {} for s in self.STREAMS}

    def emit(self, stream, build, reads=(), writes=(), hazards=(), dma=False,
             acc=False, dsem=None):
        if dma:
            sem = dsem
            inc = 16
        else:
            sem = stream
            inc = 1
        waits = {}
        if dma and self.cnt[sem] > 0 and \
                self.waited[stream].get(sem, -1) < self.cnt[sem]:
            waits[sem] = self.cnt[sem]

        def need(dep):
            for s, v in dep.items():
                if stream == "T" and s == "T" and not dma:
                    continue
                if self.waited[stream].get(s, -1) >= v:
                    continue
                if waits.get(s, -1) < v:
                    waits[s] = v

        for b in reads:
            need(b.writers)
        if not acc:
            for b in writes:
                need(b.writers)
                need(b.readers)
            for b in hazards:
                need(b.writers)
                need(b.readers)
        nv = self.cnt[sem] + inc
        self.cnt[sem] = nv
        for s, v in waits.items():
            assert v <= self.cnt[s], f"wait on future {s}:{v}>{self.cnt[s]}"
            if self.waited[stream].get(s, -1) < v:
                self.waited[stream][s] = v
        for b in reads:
            if b.readers.get(sem, -1) < nv:
                b.readers[sem] = nv
        for b in writes:
            if acc:
                b.writers[sem] = max(b.writers.get(sem, -1), nv)
            else:
                b.writers = {sem: nv}
                b.readers = {}
        self.ir[stream].append(
            Ins([(self.sems[s], v) for s, v in sorted(waits.items())],
                build, [(self.sems[sem], inc)]))

    def barrier(self, stream, sem_name, value):
        self.ir[stream].append(Ins([(self.sems[sem_name], value)], None, []))

    def replay(self, stream, eng):
        for ins in self.ir[stream]:
            for sem, v in ins.waits:
                eng.wait_ge(sem, v)
            if ins.build is not None:
                bi = ins.build(eng)
                for sem, d in ins.incs:
                    bi.then_inc(sem, d)



def build_program(n_tok, dbg_x2=False):
    nst = n_tok // ST
    nc = bass.Bass("TRN2", target_bir_lowering=False)

    x_d = nc.dram_tensor("x", [n_tok, D], F32, kind="ExternalInput").ap()
    y_d = nc.dram_tensor("y", [n_tok, D], F32, kind="ExternalOutput").ap()

    def din(name, shape, dt=F32):
        return nc.dram_tensor(name, shape, dt, kind="ExternalInput").ap()

    wq_d = din("wq_c", [4, 128, 16, 512])
    wk_d = din("wk_c", [4, 128, 16, 512])
    wv_d = din("wv_c", [4, 128, 16, 512])
    wo_d = din("wo_c", [4, 128, 16, 512])
    wup_d = din("wup_c", [6, 2, 128, 16, 512])
    wsw_d = din("wsw_c", [6, 2, 128, 8, 512])
    wdn_d = din("wdn_c", [6, 4, 128, 8, 512])
    wsu_d = din("wsu_c", [4, 128, 16, 512])
    wsa_d = din("wsa_c", [2, 128, 8, 512])
    wsd_d = din("wsd_c", [4, 128, 16, 512])
    wg_d = din("wg_c", [128, 16, 16])
    bupc_d = din("bupc", [128, 48])
    bswc_d = din("bswc", [128, 48])
    bsac_d = din("bsac", [128, 8])
    bsuc_d = din("bsuc", [128, 16])
    bdn_d = din("bdn2d", [6, D])
    bsd_d = din("bsd_r", [1, D])
    bg_d = din("bg2d", [2, 8])
    ones_d = din("ones_c", [1, 128], BF16)
    sel8_d = din("sel8_c", [8, 1024], BF16)
    selg_d = din("selg_c", [2, 256], BF16)
    ident_d = din("ident_c", [128, 128], BF16)

    x2scr_d = nc.dram_tensor("x2scr", [n_tok, D], F32,
                             kind="ExternalOutput" if dbg_x2 else "Internal").ap()
    scr = {}
    for nm, shp in (("wq", [4, 128, 16, 512]), ("wk", [4, 128, 16, 512]),
                    ("wv", [4, 128, 16, 512]), ("wo", [4, 128, 16, 512]),
                    ("wup", [6, 2, 128, 16, 512]), ("wsw", [6, 2, 128, 8, 512]),
                    ("wdn", [6, 4, 128, 8, 512]), ("wsu", [4, 128, 16, 512]),
                    ("wsa", [2, 128, 8, 512]), ("wsd", [4, 128, 16, 512])):
        scr[nm] = nc.dram_tensor("scr_" + nm, shp, BF16, kind="Internal").ap()

    from contextlib import ExitStack
    with ExitStack() as ctx:
        e = ctx.enter_context

        fmajT = e(nc.sbuf_tensor("fmajT", [128, 16, ST], BF16))
        qkv = e(nc.sbuf_tensor("qkv", [128, 3 * NT * D], BF16))
        othsu = e(nc.sbuf_tensor("othsu", [128, 16, ST], BF16))
        o32 = e(nc.sbuf_tensor("o32", [128, D], F32))
        lnbf = e(nc.sbuf_tensor("lnbf", [128, D], BF16))
        obf = lnbf
        scoresb = e(nc.sbuf_tensor("scoresb", [128, NT, 256], F32))
        scr_v = e(nc.sbuf_tensor("scr_v", [128, 128], BF16))
        shuT = e(nc.sbuf_tensor("shuT", [128, 8, ST], BF16))
        swF = e(nc.sbuf_tensor("swF", [128, 16, ST], BF16))
        xst = e(nc.sbuf_tensor("xst", [128, 4, 512], F32))
        x2st = e(nc.sbuf_tensor("x2st", [128, 4, 512], F32))
        outst = e(nc.sbuf_tensor("outst", [128, 2, 512], F32))
        xt1bf = e(nc.sbuf_tensor("xt1bf", [128, 2, 512], BF16))
        wA = [e(nc.sbuf_tensor(f"wA{i}", [128, 16, 512], BF16)) for i in range(2)]
        wB = [e(nc.sbuf_tensor(f"wB{i}", [128, 8, 512], BF16)) for i in range(2)]
        bw_sb = [e(nc.sbuf_tensor(f"bw{i}", [128, 512], BF16)) for i in range(2)]
        wg_sb = e(nc.sbuf_tensor("wg_sb", [128, 16, 16], BF16))
        bupc = e(nc.sbuf_tensor("bupc_s", [128, 48], F32))
        bswc = e(nc.sbuf_tensor("bswc_s", [128, 48], F32))
        bsac = e(nc.sbuf_tensor("bsac_s", [128, 8], F32))
        bsuc = e(nc.sbuf_tensor("bsuc_s", [128, 16], F32))
        bdn2d = e(nc.sbuf_tensor("bdn2d_s", [6, D], BF16))
        bsd_r = e(nc.sbuf_tensor("bsd_s", [1, D], BF16))
        bg2d = e(nc.sbuf_tensor("bg2d_s", [2, 8], BF16))
        ones_s = e(nc.sbuf_tensor("ones_s", [1, 128], BF16))
        sel8 = e(nc.sbuf_tensor("sel8_s", [8, 1024], BF16))
        selg = e(nc.sbuf_tensor("selg_s", [2, 256], BF16))
        ident = e(nc.sbuf_tensor("ident_s", [128, 128], BF16))
        stats = e(nc.sbuf_tensor("stats", [128, NT, 4, 6], F32))
        mv = e(nc.sbuf_tensor("mv", [128, 2], F32))
        lncols = e(nc.sbuf_tensor("lncols", [128, 4], F32))
        m2r2 = e(nc.sbuf_tensor("m2r2", [128, NT, 2], F32))
        g8 = e(nc.sbuf_tensor("g8", [128, NT, 8], F32))
        rtmp = e(nc.sbuf_tensor("rtmp", [128, 3, 8], F32))
        rcol = e(nc.sbuf_tensor("rcol", [128, 8], F32))
        w_r = e(nc.sbuf_tensor("w_r", [128, NT, 8], F32))
        wbf = e(nc.sbuf_tensor("wbf", [128, 8], BF16))
        wT = e(nc.sbuf_tensor("wT", [8, ST], BF16))

        pm = [e(nc.psum_tensor(f"pm{i}", [128, 512], F32)) for i in range(7)]
        ptp = e(nc.psum_tensor("ptp", [128, 1024], BF16))

        dma_fams = (["cst", "xld0", "xld1", "xld2", "xld3",
                     "x2ld0", "x2ld1", "x2ld2", "x2ld3",
                     "wA0", "wA1", "wB0", "wB1",
                     "scst0", "scst1", "scst2",
                     "sx0", "sx1", "sy0", "sy1"]
                    + [f"tf{i}" for i in range(4)]
                    + [f"to{i}" for i in range(4)])
        sems = {k: e(nc.semaphore(f"sem_{k}"))
                for k in ["T", "V", "S", "G"] + dma_fams}
        P = Prog(nc, sems)

        b_fmajT = Buf("fmajT")
        b_qkv = [[Buf(f"qkv{w}_{t}") for t in range(NT)] for w in range(3)]
        b_gf = [Buf(f"gF{i}") for i in range(6)]
        b_oT = [Buf(f"oT{t}") for t in range(NT)]
        b_hF, b_suF = Buf("hF"), Buf("suF")
        b_o32, b_lnbf = Buf("o32"), Buf("lnbf")
        b_obf = b_lnbf
        b_scores = [Buf(f"sc{t}") for t in range(NT)]
        b_shuT, b_swF = Buf("shuT"), Buf("swF")
        b_xst = [Buf(f"xst{i}") for i in range(4)]
        b_x2st = [Buf(f"x2st{i}") for i in range(4)]
        b_outst = [Buf("out0"), Buf("out1")]
        b_xt1bf = [Buf("xt1b0"), Buf("xt1b1")]
        b_wA = [Buf("wA0"), Buf("wA1")]
        b_wB = [Buf("wB0"), Buf("wB1")]
        b_bw = [Buf("bw0"), Buf("bw1")]
        b_pm = [Buf(f"pm{i}") for i in range(7)]
        b_ptp = Buf("ptp")
        b_stats = [Buf(f"st{t}") for t in range(NT)]
        b_mv, b_lncols, b_m2r2 = Buf("mv"), Buf("lncols"), Buf("m2r2")
        b_g8 = [Buf(f"g8_{t}") for t in range(NT)]
        b_rtmp, b_rcol = Buf("rtmp"), Buf("rcol")
        b_w = [Buf(f"w{t}") for t in range(NT)]
        b_wbf, b_wT = Buf("wbf"), Buf("wT")
        b_x2scr = Buf("x2scr")
        b_scr = {k: Buf("scr_" + k) for k in scr}
        b_const = Buf("const")

        al_w = {0: [b_gf[0], b_gf[1]], 1: [b_gf[2], b_gf[3]],
                2: [b_gf[4], b_gf[5]]}
        al_gf = {e_: [b_qkv[e_ // 2][2 * (e_ % 2)],
                      b_qkv[e_ // 2][2 * (e_ % 2) + 1]] for e_ in range(6)}
        al_oT = [b_hF, b_suF]
        al_h = list(b_oT)
        al_su = list(b_oT)

        def gf_view(ei):
            return qkv[:, ei * 4096:(ei + 1) * 4096].rearrange(
                "p (k c) -> p k c", c=512)

        rots = {"tf": 0, "to": 0, "scst": 0, "sx": 0, "sy": 0}

        def rot(fam, n):
            i = rots[fam] % n
            rots[fam] += 1
            return f"{fam}{i}"

        def dma(stream, out_ap, in_ap, dsem, reads=(), writes=(), hazards=(),
                acc=False, transpose=False):
            if transpose:
                bld = (lambda eng, o=out_ap, i=in_ap:
                       eng.dma_start_transpose(o, i))
            else:
                bld = (lambda eng, o=out_ap, i=in_ap:
                       eng.dma_start(out=o, in_=i))
            P.emit(stream, bld, reads=reads, writes=writes, hazards=hazards,
                   dma=True, acc=acc, dsem=dsem)

        def mm(psum_ap, lhsT_ap, rhs_ap, start, stop, reads, pbuf):
            bld = (lambda eng, o=psum_ap, l=lhsT_ap, r=rhs_ap, a=start, b=stop:
                   eng.matmul(o, l, r, start=a, stop=b))
            P.emit("T", bld, reads=reads, writes=[pbuf], acc=not start)

        def load_w(stidx, slot_sb, slot_buf, fam, f32_ap, scr_ap, scr_buf):
            if stidx == 0:
                dma("G", slot_sb[:], f32_ap, fam, reads=[b_const],
                    writes=[slot_buf])
                dma("G", scr_ap, slot_sb[:], rot("scst", 3), reads=[slot_buf],
                    writes=[scr_buf], acc=True)
            else:
                dma("G", slot_sb[:], scr_ap, fam, reads=[scr_buf],
                    writes=[slot_buf])

        ring = {"A": 0, "B": 0}

        def nextA():
            i = ring["A"] % 2
            ring["A"] += 1
            return wA[i], b_wA[i], f"wA{i}"

        def nextB():
            i = ring["B"] % 2
            ring["B"] += 1
            return wB[i], b_wB[i], f"wB{i}"

        pmrot = [0]

        def next_pm():
            i = pmrot[0] % 6
            pmrot[0] += 1
            return i

        for dst, src in ((bupc, bupc_d), (bswc, bswc_d), (bsac, bsac_d),
                         (bsuc, bsuc_d), (ones_s, ones_d), (sel8, sel8_d),
                         (selg, selg_d), (ident, ident_d)):
            dma("Y", dst[:], src, "cst", writes=[b_const], acc=True)
        for dst, src in ((bdn2d, bdn_d), (bsd_r, bsd_d), (bg2d, bg_d),
                         (wg_sb, wg_d)):
            dma("G", dst[:], src, "cst", writes=[b_const], acc=True)

        def ln_tile(t, src_chunk_dma, rstd_ap, negmr_ap, b_rstd, rstd_acc):
            for oc in range(4):
                sl = oc
                src_chunk_dma(oc, sl)
                P.emit("V", lambda eng, t=t, oc=oc, sl=sl: eng.bn_stats(
                    stats[:, t, oc, :], xst[:, sl, :]),
                    reads=[b_xst[sl]], writes=[b_stats[t]], acc=(oc > 0))
            P.emit("V", lambda eng, t=t: eng.bn_aggr(
                mv[:], stats[:, t, :, :].rearrange("p a b -> p (a b)")),
                reads=[b_stats[t]], writes=[b_mv])
            P.emit("V", lambda eng: eng.tensor_scalar(
                lncols[:, 0:1], mv[:, 1:2], EPS, None, ALU.add),
                reads=[b_mv], writes=[b_lncols])
            P.emit("S", lambda eng: eng.activation(
                lncols[:, 3:4], lncols[:, 0:1], AF.Ln),
                reads=[b_lncols], writes=[b_lncols], acc=True)
            P.emit("S", lambda eng, r=rstd_ap: eng.activation(
                r, lncols[:, 3:4], AF.Exp, scale=-0.5),
                reads=[b_lncols], writes=[b_rstd], acc=rstd_acc)
            P.emit("V", lambda eng, r=rstd_ap, n=negmr_ap: eng.tensor_scalar(
                n, mv[:, 0:1], r, -1.0, ALU.mult, ALU.mult),
                reads=[b_mv, b_rstd], writes=[b_rstd], acc=True)

        def emit_supertile(stidx):
            row0 = stidx * ST

            for t in range(NT):
                r0 = row0 + t * 128

                def xload(oc, sl, r0=r0):
                    dma("S", xst[:, sl, :],
                        x_d[r0:r0 + 128, oc * 512:(oc + 1) * 512],
                        f"xld{sl}", writes=[b_xst[sl]])

                ln_tile(t, xload, lncols[:, 1:2], lncols[:, 2:3], b_lncols, True)
                for oc in range(4):
                    P.emit("S", lambda eng, oc=oc: eng.activation(
                        lnbf[:, oc * 512:(oc + 1) * 512], xst[:, oc, :],
                        AF.Identity, bias=lncols[:, 2:3],
                        scale=lncols[:, 1:2]),
                        reads=[b_xst[oc], b_lncols], writes=[b_lnbf],
                        acc=(oc > 0))
                for kc in range(16):
                    dma("Y", fmajT[:, kc, t * 128:(t + 1) * 128],
                        lnbf[:, kc * 128:(kc + 1) * 128],
                        rot("tf", 4), reads=[b_lnbf], writes=[b_fmajT],
                        acc=not (t == 0 and kc == 0), transpose=True)

            def attn_proj(wi, wname, wd):
                for oc in range(4):
                    slot, sb, fam = nextA()
                    load_w(stidx, slot, sb, fam, wd[oc], scr[wname][oc],
                           b_scr[wname])
                    for t in range(NT):
                        pi = next_pm()
                        for kc in range(16):
                            mm(pm[pi][:], fmajT[:, kc, t * 128:(t + 1) * 128],
                               slot[:, kc, :], kc == 0, kc == 15,
                               [b_fmajT, sb], b_pm[pi])
                        c0 = wi * NT * D + t * D + oc * 512
                        P.emit("S", lambda eng, pi=pi, c0=c0: eng.copy(
                            qkv[:, c0:c0 + 512], pm[pi][:]),
                            reads=[b_pm[pi]], writes=[b_qkv[wi][t]],
                            hazards=al_w[wi], acc=(oc > 0))

            def attn_scores(t):
                qc = 0 * NT * D + t * D
                kcol = 1 * NT * D + t * D
                for h in range(NH):
                    for k in range(NH):
                        P.emit("V", lambda eng, h=h, k=k, qc=qc, kcol=kcol,
                               t=t: eng.scalar_tensor_tensor(
                                   scr_v[:],
                                   qkv[:, qc + h * 128:qc + (h + 1) * 128],
                                   1.0,
                                   qkv[:, kcol + k * 128:kcol + (k + 1) * 128],
                                   ALU.mult, ALU.mult,
                                   accum_out=scoresb[:, t, h * 16 + k:
                                                     h * 16 + k + 1]),
                               reads=[b_qkv[0][t], b_qkv[1][t]],
                               writes=[b_scores[t]],
                               acc=not (h == 0 and k == 0))

            def attn_o(t):
                vc = 2 * NT * D + t * D
                for h in range(NH):
                    osl = o32[:, h * 128:(h + 1) * 128]
                    for k in range(NH):
                        col = scoresb[:, t, h * 16 + k:h * 16 + k + 1]
                        vsl = qkv[:, vc + k * 128:vc + (k + 1) * 128]
                        if k == 0:
                            P.emit("V", lambda eng, c=col, v=vsl, o=osl:
                                   eng.tensor_scalar(o, v, c, None, ALU.mult),
                                   reads=[b_scores[t], b_qkv[2][t]],
                                   writes=[b_o32], acc=(h > 0))
                        else:
                            P.emit("V", lambda eng, c=col, v=vsl, o=osl:
                                   eng.scalar_tensor_tensor(
                                       o, v, c, o, ALU.mult, ALU.add),
                                   reads=[b_scores[t], b_qkv[2][t], b_o32],
                                   writes=[b_o32], acc=True)
                P.emit("S", lambda eng: eng.copy(obf[:], o32[:]),
                       reads=[b_o32], writes=[b_obf])
                for kc in range(16):
                    dma("Y", othsu[:, kc, t * 128:(t + 1) * 128],
                        obf[:, kc * 128:(kc + 1) * 128],
                        rot("to", 4), reads=[b_obf], writes=[b_oT[t]],
                        hazards=al_oT, acc=(kc > 0), transpose=True)

            attn_proj(0, "wq", wq_d)
            attn_proj(1, "wk", wk_d)
            for t in range(NT - 1):
                attn_scores(t)
            attn_proj(2, "wv", wv_d)
            attn_o(0)
            attn_scores(NT - 1)
            for t in range(1, NT):
                attn_o(t)

            for oc in range(4):
                slot, sb, fam = nextA()
                load_w(stidx, slot, sb, fam, wo_d[oc], scr["wo"][oc],
                       b_scr["wo"])
                for t in range(NT):
                    r0 = row0 + t * 128
                    pi = next_pm()
                    for kc in range(16):
                        mm(pm[pi][:], othsu[:, kc, t * 128:(t + 1) * 128],
                           slot[:, kc, :], kc == 0, kc == 15,
                           [b_oT[t], sb], b_pm[pi])
                    sl = t
                    dma("S", xst[:, sl, :],
                        x_d[r0:r0 + 128, oc * 512:(oc + 1) * 512],
                        f"xld{sl}", writes=[b_xst[sl]])
                    P.emit("V", lambda eng, pi=pi, sl=sl: eng.tensor_add(
                        x2st[:, sl, :], xst[:, sl, :], pm[pi][:]),
                        reads=[b_xst[sl], b_pm[pi]], writes=[b_x2st[sl]])
                    P.emit("V", lambda eng, t=t, oc=oc, sl=sl: eng.bn_stats(
                        stats[:, t, oc, :], x2st[:, sl, :]),
                        reads=[b_x2st[sl]], writes=[b_stats[t]], acc=(oc > 0))
                    dma("S", x2scr_d[r0:r0 + 128, oc * 512:(oc + 1) * 512],
                        x2st[:, sl, :], rot("sx", 2), reads=[b_x2st[sl]],
                        writes=[b_x2scr], acc=not (oc == 0 and t == 0))

            for t in range(NT):
                r0 = row0 + t * 128
                P.emit("V", lambda eng, t=t: eng.bn_aggr(
                    mv[:], stats[:, t, :, :].rearrange("p a b -> p (a b)")),
                    reads=[b_stats[t]], writes=[b_mv])
                P.emit("V", lambda eng: eng.tensor_scalar(
                    lncols[:, 0:1], mv[:, 1:2], EPS, None, ALU.add),
                    reads=[b_mv], writes=[b_lncols])
                P.emit("S", lambda eng: eng.activation(
                    lncols[:, 3:4], lncols[:, 0:1], AF.Ln),
                    reads=[b_lncols], writes=[b_lncols], acc=True)
                P.emit("S", lambda eng, t=t: eng.activation(
                    m2r2[:, t, 0:1], lncols[:, 3:4], AF.Exp, scale=-0.5),
                    reads=[b_lncols], writes=[b_m2r2], acc=(t > 0))
                P.emit("V", lambda eng, t=t: eng.tensor_scalar(
                    m2r2[:, t, 1:2], mv[:, 0:1], m2r2[:, t, 0:1], -1.0,
                    ALU.mult, ALU.mult),
                    reads=[b_mv, b_m2r2], writes=[b_m2r2], acc=True)
                for oc in range(4):
                    sl = oc
                    dma("S", x2st[:, sl, :],
                        x2scr_d[r0:r0 + 128, oc * 512:(oc + 1) * 512],
                        f"x2ld{sl}", reads=[b_x2scr], writes=[b_x2st[sl]])
                    P.emit("S", lambda eng, t=t, oc=oc, sl=sl: eng.activation(
                        lnbf[:, oc * 512:(oc + 1) * 512], x2st[:, sl, :],
                        AF.Identity, bias=m2r2[:, t, 1:2],
                        scale=m2r2[:, t, 0:1]),
                        reads=[b_x2st[sl], b_m2r2], writes=[b_lnbf],
                        acc=(oc > 0))
                for kc in range(16):
                    dma("Y", fmajT[:, kc, t * 128:(t + 1) * 128],
                        lnbf[:, kc * 128:(kc + 1) * 128],
                        rot("tf", 4), reads=[b_lnbf], writes=[b_fmajT],
                        acc=not (t == 0 and kc == 0), transpose=True)

            for j in range(2):
                for t in range(NT):
                    for kc in range(16):
                        mm(pm[6][:, 0:8], fmajT[:, kc, t * 128:(t + 1) * 128],
                           wg_sb[:, kc, j * 8:(j + 1) * 8], kc == 0, False,
                           [b_fmajT, b_const], b_pm[6])
                    mm(pm[6][:, 0:8], selg[0:2, j * 128:(j + 1) * 128],
                       bg2d[0:2, :], False, True, [b_const], b_pm[6])
                    P.emit("S", lambda eng, t=t: eng.copy(
                        g8[:, t, :], pm[6][:, 0:8]),
                        reads=[b_pm[6]], writes=[b_g8[t]])
                    gap = g8[:, t, :]
                    P.emit("V", lambda eng, g=gap: eng.tensor_reduce(
                        rcol[:, 0:1], g, AX.X, ALU.min),
                        reads=[b_g8[t]], writes=[b_rcol])
                    P.emit("V", lambda eng, g=gap: eng.tensor_scalar(
                        rtmp[:, 0, :], g, rcol[:, 0:1], None, ALU.is_equal),
                        reads=[b_g8[t], b_rcol], writes=[b_rtmp])
                    P.emit("V", lambda eng, g=gap: eng.scalar_tensor_tensor(
                        rtmp[:, 1, :], rtmp[:, 0, :], 1e30, g,
                        ALU.mult, ALU.add),
                        reads=[b_g8[t], b_rtmp], writes=[b_rtmp], acc=True)
                    P.emit("V", lambda eng: eng.tensor_reduce(
                        rcol[:, 1:2], rtmp[:, 1, :], AX.X, ALU.min),
                        reads=[b_rtmp], writes=[b_rcol], acc=True)
                    P.emit("V", lambda eng, g=gap: eng.tensor_scalar(
                        rtmp[:, 1, :], g, rcol[:, 1:2], None, ALU.is_gt),
                        reads=[b_g8[t], b_rcol], writes=[b_rtmp], acc=True)
                    P.emit("V", lambda eng, g=gap: eng.reduce_max(
                        rcol[:, 2:3], g, AX.X, negate=True),
                        reads=[b_g8[t]], writes=[b_rcol], acc=True)
                    P.emit("S", lambda eng, g=gap: eng.activation(
                        rtmp[:, 2, :], g, AF.Exp, bias=rcol[:, 2:3]),
                        reads=[b_g8[t], b_rcol], writes=[b_rtmp], acc=True)
                    P.emit("V", lambda eng: eng.tensor_mul(
                        rtmp[:, 0, :], rtmp[:, 2, :], rtmp[:, 1, :]),
                        reads=[b_rtmp], writes=[b_rtmp], acc=True)
                    P.emit("V", lambda eng: eng.tensor_reduce(
                        rcol[:, 3:4], rtmp[:, 0, :], AX.X, ALU.add),
                        reads=[b_rtmp], writes=[b_rcol], acc=True)
                    P.emit("V", lambda eng: eng.reciprocal(
                        rcol[:, 4:5], rcol[:, 3:4]),
                        reads=[b_rcol], writes=[b_rcol], acc=True)
                    P.emit("V", lambda eng, t=t: eng.tensor_scalar(
                        w_r[:, t, :], rtmp[:, 0, :], rcol[:, 4:5], None,
                        ALU.mult),
                        reads=[b_rtmp, b_rcol], writes=[b_w[t]])
                    P.emit("S", lambda eng, t=t: eng.copy(wbf[:], w_r[:, t, :]),
                           reads=[b_w[t]], writes=[b_wbf])
                    P.emit("T", lambda eng: eng.transpose(
                        ptp[0:8, 0:128], wbf[:], ident[:]),
                        reads=[b_wbf, b_const], writes=[b_ptp])
                    P.emit("S", lambda eng, t=t: eng.copy(
                        wT[0:8, t * 128:(t + 1) * 128], ptp[0:8, 0:128]),
                        reads=[b_ptp], writes=[b_wT], acc=(t > 0))

                for ex in range(6):
                    bws = ex % 2
                    mm(pm[6][:, 0:512], sel8[0:8, ex * 128:(ex + 1) * 128],
                       wT[0:8, :], True, True, [b_const, b_wT], b_pm[6])
                    P.emit("S", lambda eng, s=bws: eng.copy(
                        bw_sb[s][:], pm[6][:, 0:512]),
                        reads=[b_pm[6]], writes=[b_bw[bws]])
                    for och in range(2):
                        slot, sb, fam = nextA()
                        load_w(stidx, slot, sb, fam, wup_d[ex, och],
                               scr["wup"][ex, och], b_scr["wup"])
                        for ofc in range(4):
                            pi = next_pm()
                            for kc in range(16):
                                mm(pm[pi][:],
                                   slot[:, kc, ofc * 128:(ofc + 1) * 128],
                                   fmajT[:, kc, :], kc == 0, kc == 15,
                                   [b_fmajT, sb], b_pm[pi])
                            cc = och * 4 + ofc
                            P.emit("V", lambda eng, pi=pi, cc=cc, ex=ex:
                                   eng.tensor_scalar(
                                       othsu[:, cc, :], pm[pi][:],
                                       bupc[:, ex * 8 + cc:ex * 8 + cc + 1],
                                       None, ALU.add),
                                   reads=[b_pm[pi], b_const],
                                   writes=[b_hF], hazards=al_h,
                                   acc=(och * 4 + ofc > 0))
                    for och in range(2):
                        slot, sb, fam = nextB()
                        load_w(stidx, slot, sb, fam, wsw_d[ex, och],
                               scr["wsw"][ex, och], b_scr["wsw"])
                        for ofc in range(4):
                            pi = next_pm()
                            for kc in range(8):
                                mm(pm[pi][:],
                                   slot[:, kc, ofc * 128:(ofc + 1) * 128],
                                   othsu[:, kc, :], kc == 0, kc == 7,
                                   [b_hF, sb], b_pm[pi])
                            cc = och * 4 + ofc
                            P.emit("S", lambda eng, pi=pi, cc=cc, ex=ex:
                                   eng.activation(
                                       othsu[:, 8 + cc, :], pm[pi][:],
                                       AF.Silu,
                                       bias=bswc[:, ex * 8 + cc:
                                                 ex * 8 + cc + 1]),
                                   reads=[b_pm[pi], b_const],
                                   writes=[b_suF], hazards=al_su,
                                   acc=(och * 4 + ofc > 0))
                    gc0 = ex * 4096
                    for cc in range(8):
                        P.emit("V", lambda eng, cc=cc: eng.tensor_mul(
                            othsu[:, 8 + cc, :], othsu[:, 8 + cc, :],
                            othsu[:, cc, :]),
                            reads=[b_hF, b_suF], writes=[b_suF], acc=True)
                        P.emit("V", lambda eng, cc=cc, gc0=gc0, s=bws:
                               eng.tensor_mul(
                                   qkv[:, gc0 + cc * 512:gc0 + (cc + 1) * 512],
                                   othsu[:, 8 + cc, :], bw_sb[s][:]),
                               reads=[b_suF, b_bw[bws]], writes=[b_gf[ex]],
                               hazards=al_gf[ex], acc=(cc > 0))

                for c in range(2):
                    for q2 in range(2):
                        slot, sb, fam = nextA()
                        ocu = c * 2 + q2
                        load_w(stidx, slot, sb, fam, wsu_d[ocu],
                               scr["wsu"][ocu], b_scr["wsu"])
                        for ofc in range(4):
                            pi = next_pm()
                            for kc in range(16):
                                mm(pm[pi][:],
                                   slot[:, kc, ofc * 128:(ofc + 1) * 128],
                                   fmajT[:, kc, :], kc == 0, kc == 15,
                                   [b_fmajT, sb], b_pm[pi])
                            cc = q2 * 4 + ofc
                            P.emit("V", lambda eng, pi=pi, cc=cc, c=c:
                                   eng.tensor_scalar(
                                       shuT[:, cc, :], pm[pi][:],
                                       bsuc[:, c * 8 + cc:c * 8 + cc + 1],
                                       None, ALU.add),
                                   reads=[b_pm[pi], b_const], writes=[b_shuT],
                                   acc=(q2 * 4 + ofc > 0))
                    for och in range(2):
                        slot, sb, fam = nextB()
                        load_w(stidx, slot, sb, fam, wsa_d[och],
                               scr["wsa"][och], b_scr["wsa"])
                        for ofc in range(4):
                            pi = next_pm()
                            for kc in range(8):
                                mm(pm[pi][:],
                                   slot[:, kc, ofc * 128:(ofc + 1) * 128],
                                   shuT[:, kc, :], kc == 0, kc == 7,
                                   [b_shuT, sb], b_pm[pi])
                            cc = och * 4 + ofc
                            P.emit("S", lambda eng, pi=pi, cc=cc:
                                   eng.activation(
                                       othsu[:, cc, :], pm[pi][:], AF.Silu,
                                       bias=bsac[:, cc:cc + 1]),
                                   reads=[b_pm[pi], b_const], writes=[b_hF],
                                   hazards=al_h, acc=(och * 4 + ofc > 0))
                    for cc in range(8):
                        P.emit("V", lambda eng, cc=cc, c=c: eng.tensor_mul(
                            swF[:, c * 8 + cc, :], othsu[:, cc, :],
                            shuT[:, cc, :]),
                            reads=[b_hF, b_shuT], writes=[b_swF],
                            acc=not (c == 0 and cc == 0))

                for oc in range(4):
                    pis = [0, 1, 2, 3] if oc % 2 == 0 else [4, 5, 6, 0]
                    for t in range(NT):
                        r0 = row0 + t * 128
                        dma("S", x2st[:, t, :],
                            x2scr_d[r0:r0 + 128, oc * 512:(oc + 1) * 512],
                            f"x2ld{t}", reads=[b_x2scr], writes=[b_x2st[t]])
                    for ei in range(6):
                        slot, sb, fam = nextB()
                        load_w(stidx, slot, sb, fam, wdn_d[ei, oc],
                               scr["wdn"][ei, oc], b_scr["wdn"])
                        gv = gf_view(ei)
                        for t in range(NT):
                            for kc in range(8):
                                mm(pm[pis[t]][:],
                                   gv[:, kc, t * 128:(t + 1) * 128],
                                   slot[:, kc, :], ei == 0 and kc == 0, False,
                                   [b_gf[ei], sb], b_pm[pis[t]])
                    slot, sb, fam = nextA()
                    load_w(stidx, slot, sb, fam, wsd_d[oc], scr["wsd"][oc],
                           b_scr["wsd"])
                    for t in range(NT):
                        for kc in range(16):
                            mm(pm[pis[t]][:],
                               swF[:, kc, t * 128:(t + 1) * 128],
                               slot[:, kc, :], False, False,
                               [b_swF, sb], b_pm[pis[t]])
                    for t in range(NT):
                        mm(pm[pis[t]][:], wT[0:6, t * 128:(t + 1) * 128],
                           bdn2d[0:6, oc * 512:(oc + 1) * 512], False, False,
                           [b_wT, b_const], b_pm[pis[t]])
                        mm(pm[pis[t]][:], ones_s[0:1, :],
                           bsd_r[0:1, oc * 512:(oc + 1) * 512], False, True,
                           [b_const], b_pm[pis[t]])
                    for t in range(NT):
                        r0 = row0 + t * 128
                        pi = pis[t]
                        if j == 0:
                            sl = t % 2
                            P.emit("S", lambda eng, t=t, sl=sl: eng.activation(
                                xt1bf[:, sl, :], x2st[:, t, :], AF.Identity,
                                bias=m2r2[:, t, 1:2], scale=m2r2[:, t, 0:1]),
                                reads=[b_x2st[t], b_m2r2],
                                writes=[b_xt1bf[sl]])
                            P.emit("V", lambda eng, pi=pi, sl=sl:
                                   eng.tensor_add(
                                       xt1bf[:, sl, :], xt1bf[:, sl, :],
                                       pm[pi][:]),
                                   reads=[b_pm[pi], b_xt1bf[sl]],
                                   writes=[b_xt1bf[sl]], acc=True)
                            for kq in range(4):
                                dma("Y", fmajT[:, oc * 4 + kq,
                                               t * 128:(t + 1) * 128],
                                    xt1bf[:, sl, kq * 128:(kq + 1) * 128],
                                    rot("tf", 4),
                                    reads=[b_xt1bf[sl]], writes=[b_fmajT],
                                    acc=not (oc == 0 and t == 0 and kq == 0),
                                    transpose=True)
                        else:
                            sl = t % 2
                            P.emit("S", lambda eng, t=t, sl=sl: eng.activation(
                                outst[:, sl, :], x2st[:, t, :], AF.Identity,
                                bias=m2r2[:, t, 1:2], scale=m2r2[:, t, 0:1]),
                                reads=[b_x2st[t], b_m2r2],
                                writes=[b_outst[sl]])
                            P.emit("V", lambda eng, pi=pi, sl=sl:
                                   eng.tensor_add(
                                       outst[:, sl, :], outst[:, sl, :],
                                       pm[pi][:]),
                                   reads=[b_pm[pi], b_outst[sl]],
                                   writes=[b_outst[sl]], acc=True)
                            P.emit("V", lambda eng, t=t, sl=sl: eng.tensor_add(
                                outst[:, sl, :], outst[:, sl, :],
                                x2st[:, t, :]),
                                reads=[b_x2st[t], b_outst[sl]],
                                writes=[b_outst[sl]], acc=True)
                            dma("S", y_d[r0:r0 + 128, oc * 512:(oc + 1) * 512],
                                outst[:, sl, :], rot("sy", 2),
                                reads=[b_outst[sl]])

        for stidx in range(nst):
            emit_supertile(stidx)

        P.barrier("S", "sy0", P.cnt["sy0"])
        P.barrier("S", "sy1", P.cnt["sy1"])
        P.barrier("Y", "sy0", P.cnt["sy0"])
        P.barrier("Y", "sy1", P.cnt["sy1"])

        with nc.Block() as block:
            @block.tensor
            def _(eng):
                P.replay("T", eng)

            @block.vector
            def _(eng):
                P.replay("V", eng)

            @block.scalar
            def _(eng):
                P.replay("S", eng)

            @block.sync
            def _(eng):
                P.replay("Y", eng)

            @block.gpsimd
            def _(eng):
                P.replay("G", eng)

    return nc



def _chunkK(w):
    K, F = w.shape
    return np.ascontiguousarray(
        w.reshape(K // 128, 128, F // 512, 512).transpose(2, 1, 0, 3))


def _prep_inputs(inp):
    f32 = lambda a: np.asarray(a, dtype=np.float32)
    wq = f32(inp["L0_attn_Wq"]); wk = f32(inp["L0_attn_Wk"])
    wv = f32(inp["L0_attn_Wv"]); wo = f32(inp["L0_attn_Wo"])
    wg = f32(inp["L0_ffn_Wg"]); bg = f32(inp["L0_ffn_bg"])
    wup = f32(inp["L0_ffn_Wup"]); bup = f32(inp["L0_ffn_bup"])
    wsw = f32(inp["L0_ffn_Wsw"]); bsw = f32(inp["L0_ffn_bsw"])
    wdn = f32(inp["L0_ffn_Wdn"]); bdn = f32(inp["L0_ffn_bdn"])
    wsu = f32(inp["L0_ffn_Wsu"]); bsu = f32(inp["L0_ffn_bsu"])
    wsa = f32(inp["L0_ffn_Wsa"]); bsa = f32(inp["L0_ffn_bsa"])
    wsd = f32(inp["L0_ffn_Wsd"]); bsd = f32(inp["L0_ffn_bsd"])

    bf = ml_dtypes.bfloat16
    sel8 = np.zeros((8, 1024), dtype=bf)
    for ei in range(8):
        sel8[ei, ei * 128:(ei + 1) * 128] = 1
    selg = np.zeros((2, 256), dtype=bf)
    for ji in range(2):
        selg[ji, ji * 128:(ji + 1) * 128] = 1

    shared = {
        "wq_c": _chunkK(wq), "wk_c": _chunkK(wk), "wv_c": _chunkK(wv),
        "wo_c": _chunkK(wo),
        "wup_c": np.stack([_chunkK(wup[i]) for i in range(6)]),
        "wsw_c": np.stack([_chunkK(wsw[i]) for i in range(6)]),
        "wdn_c": np.stack([_chunkK(wdn[i]) for i in range(6)]),
        "wsu_c": _chunkK(wsu), "wsa_c": _chunkK(wsa), "wsd_c": _chunkK(wsd),
        "wg_c": np.ascontiguousarray(
            wg.reshape(2, 16, 128, 8).transpose(2, 1, 0, 3).reshape(128, 16, 16)),
        "bupc": np.ascontiguousarray(
            bup.reshape(6, 8, 128).transpose(2, 0, 1).reshape(128, 48)),
        "bswc": np.ascontiguousarray(
            bsw.reshape(6, 8, 128).transpose(2, 0, 1).reshape(128, 48)),
        "bsac": np.ascontiguousarray(bsa.reshape(8, 128).T),
        "bsuc": np.ascontiguousarray(bsu.reshape(16, 128).T),
        "bdn2d": np.ascontiguousarray(bdn),
        "bsd_r": np.ascontiguousarray(bsd[None, :]),
        "bg2d": np.ascontiguousarray(bg),
        "ones_c": np.ones((1, 128), dtype=bf),
        "sel8_c": sel8, "selg_c": selg,
        "ident_c": np.eye(128, dtype=bf),
    }
    return shared


def kernel(**inputs):
    x = np.asarray(inputs["L0_attn_x"], dtype=np.float32)
    B, N, _ = x.shape
    xf = np.ascontiguousarray(x.reshape(B * N, D))
    shared = _prep_inputs(inputs)
    nc = build_program(TPC)
    in_maps = [dict(shared, x=np.ascontiguousarray(xf[c * TPC:(c + 1) * TPC]))
               for c in range(N_CORES)]
    res = run_bass_kernel_spmd(nc, in_maps, core_ids=list(range(N_CORES)))
    y = np.concatenate([r["y"] for r in res.results], axis=0)
    return y.reshape(B, N, D)


# version 07
# speedup vs baseline: 1.0983x; 1.0983x over previous
import numpy as np
import ml_dtypes
import concourse.bass as bass
import concourse.mybir as mybir
from concourse.bass_utils import run_bass_kernel_spmd

F32 = mybir.dt.float32
BF16 = mybir.dt.bfloat16
AF = mybir.ActivationFunctionType
ALU = mybir.AluOpType
AX = mybir.AxisListType

N_CORES = 8
TPC = 4096
ST = 512
NT = 4
D = 2048
NH = 16
EPS = 1e-5



class Buf:
    __slots__ = ("name", "writers", "readers")

    def __init__(self, name):
        self.name = name
        self.writers = {}
        self.readers = {}


class Ins:
    __slots__ = ("waits", "build", "incs")

    def __init__(self, waits, build, incs):
        self.waits, self.build, self.incs = waits, build, incs


class Prog:
    STREAMS = ("T", "V", "S", "Y", "G")

    def __init__(self, nc, sems):
        self.nc = nc
        self.sems = sems
        self.cnt = {k: 0 for k in sems}
        self.ir = {s: [] for s in self.STREAMS}
        self.waited = {s: {} for s in self.STREAMS}

    def emit(self, stream, build, reads=(), writes=(), hazards=(), dma=False,
             acc=False, dsem=None, noself=False):
        if dma:
            sem = dsem
            inc = 16
        else:
            sem = stream
            inc = 1
        waits = {}
        if dma and self.cnt[sem] > 0 and \
                self.waited[stream].get(sem, -1) < self.cnt[sem]:
            waits[sem] = self.cnt[sem]

        def need(dep):
            for s, v in dep.items():
                if stream == "T" and s == "T" and not dma:
                    continue
                if noself and s == stream and not dma:
                    continue
                if self.waited[stream].get(s, -1) >= v:
                    continue
                if waits.get(s, -1) < v:
                    waits[s] = v

        for b in reads:
            need(b.writers)
        if not acc:
            for b in writes:
                need(b.writers)
                need(b.readers)
            for b in hazards:
                need(b.writers)
                need(b.readers)
        nv = self.cnt[sem] + inc
        self.cnt[sem] = nv
        for s, v in waits.items():
            assert v <= self.cnt[s], f"wait on future {s}:{v}>{self.cnt[s]}"
            if self.waited[stream].get(s, -1) < v:
                self.waited[stream][s] = v
        for b in reads:
            if b.readers.get(sem, -1) < nv:
                b.readers[sem] = nv
        for b in writes:
            if acc:
                b.writers[sem] = max(b.writers.get(sem, -1), nv)
            else:
                b.writers = {sem: nv}
                b.readers = {}
        self.ir[stream].append(
            Ins([(self.sems[s], v) for s, v in sorted(waits.items())],
                build, [(self.sems[sem], inc)]))

    def barrier(self, stream, sem_name, value):
        self.ir[stream].append(Ins([(self.sems[sem_name], value)], None, []))

    def replay(self, stream, eng):
        for ins in self.ir[stream]:
            for sem, v in ins.waits:
                eng.wait_ge(sem, v)
            if ins.build is not None:
                bi = ins.build(eng)
                for sem, d in ins.incs:
                    bi.then_inc(sem, d)



def build_program(n_tok, dbg_x2=False):
    nst = n_tok // ST
    nc = bass.Bass("TRN2", target_bir_lowering=False)

    x_d = nc.dram_tensor("x", [n_tok, D], F32, kind="ExternalInput").ap()
    y_d = nc.dram_tensor("y", [n_tok, D], F32, kind="ExternalOutput").ap()

    def din(name, shape, dt=F32):
        return nc.dram_tensor(name, shape, dt, kind="ExternalInput").ap()

    wq_d = din("wq_c", [4, 128, 16, 512])
    wk_d = din("wk_c", [4, 128, 16, 512])
    wv_d = din("wv_c", [4, 128, 16, 512])
    wo_d = din("wo_c", [4, 128, 16, 512])
    wup_d = din("wup_c", [6, 2, 128, 16, 512])
    wsw_d = din("wsw_c", [6, 2, 128, 8, 512])
    wdn_d = din("wdn_c", [6, 4, 128, 8, 512])
    wsu_d = din("wsu_c", [4, 128, 16, 512])
    wsa_d = din("wsa_c", [2, 128, 8, 512])
    wsd_d = din("wsd_c", [4, 128, 16, 512])
    wg_d = din("wg_c", [128, 16, 16])
    bupc_d = din("bupc", [128, 48])
    bswc_d = din("bswc", [128, 48])
    bsac_d = din("bsac", [128, 8])
    bsuc_d = din("bsuc", [128, 16])
    bdn_d = din("bdn2d", [6, D])
    bsd_d = din("bsd_r", [1, D])
    bg_d = din("bg2d", [2, 8])
    ones_d = din("ones_c", [1, 128], BF16)
    sel8_d = din("sel8_c", [8, 1024], BF16)
    selg_d = din("selg_c", [2, 256], BF16)
    ident_d = din("ident_c", [128, 128], BF16)

    x2scr_d = nc.dram_tensor("x2scr", [n_tok, D], F32,
                             kind="ExternalOutput" if dbg_x2 else "Internal").ap()
    scr = {}
    for nm, shp in (("wq", [4, 128, 16, 512]), ("wk", [4, 128, 16, 512]),
                    ("wv", [4, 128, 16, 512]), ("wo", [4, 128, 16, 512]),
                    ("wup", [6, 2, 128, 16, 512]), ("wsw", [6, 2, 128, 8, 512]),
                    ("wdn", [6, 4, 128, 8, 512]), ("wsu", [4, 128, 16, 512]),
                    ("wsa", [2, 128, 8, 512]), ("wsd", [4, 128, 16, 512])):
        scr[nm] = nc.dram_tensor("scr_" + nm, shp, BF16, kind="Internal").ap()

    from contextlib import ExitStack
    with ExitStack() as ctx:
        e = ctx.enter_context

        fmajT = e(nc.sbuf_tensor("fmajT", [128, 16, ST], BF16))
        qkv = e(nc.sbuf_tensor("qkv", [128, 3 * NT * D], BF16))
        othsu = e(nc.sbuf_tensor("othsu", [128, 16, ST], BF16))
        o32 = e(nc.sbuf_tensor("o32", [128, D], F32))
        lnbf = e(nc.sbuf_tensor("lnbf", [128, D], BF16))
        obf = lnbf
        scoresb = e(nc.sbuf_tensor("scoresb", [128, NT, 256], F32))
        scr_v = e(nc.sbuf_tensor("scr_v", [128, 128], BF16))
        shuT = e(nc.sbuf_tensor("shuT", [128, 8, ST], BF16))
        swF = e(nc.sbuf_tensor("swF", [128, 16, ST], BF16))
        xst = e(nc.sbuf_tensor("xst", [128, 4, 512], F32))
        x2st = e(nc.sbuf_tensor("x2st", [128, 4, 512], F32))
        outst = e(nc.sbuf_tensor("outst", [128, 2, 512], F32))
        xt1bf = e(nc.sbuf_tensor("xt1bf", [128, 2, 512], BF16))
        wA = [e(nc.sbuf_tensor(f"wA{i}", [128, 16, 512], BF16)) for i in range(2)]
        wB = [e(nc.sbuf_tensor(f"wB{i}", [128, 8, 512], BF16)) for i in range(2)]
        bw_sb = [e(nc.sbuf_tensor(f"bw{i}", [128, 512], BF16)) for i in range(2)]
        wg_sb = e(nc.sbuf_tensor("wg_sb", [128, 16, 16], BF16))
        bupc = e(nc.sbuf_tensor("bupc_s", [128, 48], F32))
        bswc = e(nc.sbuf_tensor("bswc_s", [128, 48], F32))
        bsac = e(nc.sbuf_tensor("bsac_s", [128, 8], F32))
        bsuc = e(nc.sbuf_tensor("bsuc_s", [128, 16], F32))
        bdn2d = e(nc.sbuf_tensor("bdn2d_s", [6, D], BF16))
        bsd_r = e(nc.sbuf_tensor("bsd_s", [1, D], BF16))
        bg2d = e(nc.sbuf_tensor("bg2d_s", [2, 8], BF16))
        ones_s = e(nc.sbuf_tensor("ones_s", [1, 128], BF16))
        sel8 = e(nc.sbuf_tensor("sel8_s", [8, 1024], BF16))
        selg = e(nc.sbuf_tensor("selg_s", [2, 256], BF16))
        ident = e(nc.sbuf_tensor("ident_s", [128, 128], BF16))
        stats = e(nc.sbuf_tensor("stats", [128, NT, 4, 6], F32))
        mv = e(nc.sbuf_tensor("mv", [128, 2], F32))
        lncols = e(nc.sbuf_tensor("lncols", [128, 4], F32))
        m2r2 = e(nc.sbuf_tensor("m2r2", [128, NT, 2], F32))
        g8 = e(nc.sbuf_tensor("g8", [128, NT, 8], F32))
        rtmp = e(nc.sbuf_tensor("rtmp", [128, 3, 8], F32))
        rcol = e(nc.sbuf_tensor("rcol", [128, 8], F32))
        w_r = e(nc.sbuf_tensor("w_r", [128, NT, 8], F32))
        wbf = e(nc.sbuf_tensor("wbf", [128, 8], BF16))
        wT = e(nc.sbuf_tensor("wT", [8, ST], BF16))

        pm = [e(nc.psum_tensor(f"pm{i}", [128, 512], F32)) for i in range(7)]
        ptp = e(nc.psum_tensor("ptp", [128, 1024], BF16))

        dma_fams = (["cst", "xld0", "xld1", "xld2", "xld3",
                     "x2ld0", "x2ld1", "x2ld2", "x2ld3",
                     "wA0", "wA1", "wB0", "wB1",
                     "scst0", "scst1", "scst2",
                     "sx0", "sx1", "sy0", "sy1"]
                    + [f"tf{i}" for i in range(4)]
                    + [f"to{i}" for i in range(4)])
        sems = {k: e(nc.semaphore(f"sem_{k}"))
                for k in ["T", "V", "S", "G"] + dma_fams}
        P = Prog(nc, sems)

        b_fmajT = Buf("fmajT")
        b_qkv = [[Buf(f"qkv{w}_{t}") for t in range(NT)] for w in range(3)]
        b_gf = [Buf(f"gF{i}") for i in range(6)]
        b_oT = [Buf(f"oT{t}") for t in range(NT)]
        b_hF, b_suF = Buf("hF"), Buf("suF")
        b_o32, b_lnbf = Buf("o32"), Buf("lnbf")
        b_obf = b_lnbf
        b_scores = [Buf(f"sc{t}") for t in range(NT)]
        b_shuT, b_swF = Buf("shuT"), Buf("swF")
        b_xst = [Buf(f"xst{i}") for i in range(4)]
        b_x2st = [Buf(f"x2st{i}") for i in range(4)]
        b_outst = [Buf("out0"), Buf("out1")]
        b_xt1bf = [Buf("xt1b0"), Buf("xt1b1")]
        b_wA = [Buf("wA0"), Buf("wA1")]
        b_wB = [Buf("wB0"), Buf("wB1")]
        b_bw = [Buf("bw0"), Buf("bw1")]
        b_pm = [Buf(f"pm{i}") for i in range(7)]
        b_ptp = Buf("ptp")
        b_stats = [Buf(f"st{t}") for t in range(NT)]
        b_mv, b_lncols, b_m2r2 = Buf("mv"), Buf("lncols"), Buf("m2r2")
        b_g8 = [Buf(f"g8_{t}") for t in range(NT)]
        b_rtmp, b_rcol = Buf("rtmp"), Buf("rcol")
        b_w = [Buf(f"w{t}") for t in range(NT)]
        b_wbf, b_wT = Buf("wbf"), Buf("wT")
        b_x2scr = Buf("x2scr")
        b_scr = {k: Buf("scr_" + k) for k in scr}
        b_const = Buf("const")

        al_w = {0: [b_gf[0], b_gf[1]], 1: [b_gf[2], b_gf[3]],
                2: [b_gf[4], b_gf[5]]}
        al_gf = {e_: [b_qkv[e_ // 2][2 * (e_ % 2)],
                      b_qkv[e_ // 2][2 * (e_ % 2) + 1]] for e_ in range(6)}
        al_oT = [b_hF, b_suF]
        al_h = list(b_oT)
        al_su = list(b_oT)

        def gf_view(ei):
            return qkv[:, ei * 4096:(ei + 1) * 4096].rearrange(
                "p (k c) -> p k c", c=512)

        rots = {"tf": 0, "to": 0, "scst": 0, "sx": 0, "sy": 0}

        def rot(fam, n):
            i = rots[fam] % n
            rots[fam] += 1
            return f"{fam}{i}"

        def dma(stream, out_ap, in_ap, dsem, reads=(), writes=(), hazards=(),
                acc=False, transpose=False):
            if transpose:
                bld = (lambda eng, o=out_ap, i=in_ap:
                       eng.dma_start_transpose(o, i))
            else:
                bld = (lambda eng, o=out_ap, i=in_ap:
                       eng.dma_start(out=o, in_=i))
            P.emit(stream, bld, reads=reads, writes=writes, hazards=hazards,
                   dma=True, acc=acc, dsem=dsem)

        def mm(psum_ap, lhsT_ap, rhs_ap, start, stop, reads, pbuf):
            bld = (lambda eng, o=psum_ap, l=lhsT_ap, r=rhs_ap, a=start, b=stop:
                   eng.matmul(o, l, r, start=a, stop=b))
            P.emit("T", bld, reads=reads, writes=[pbuf], acc=not start)

        def load_w(stidx, slot_sb, slot_buf, fam, f32_ap, scr_ap, scr_buf):
            if stidx == 0:
                dma("G", slot_sb[:], f32_ap, fam, reads=[b_const],
                    writes=[slot_buf])
                dma("G", scr_ap, slot_sb[:], rot("scst", 3), reads=[slot_buf],
                    writes=[scr_buf], acc=True)
            else:
                dma("G", slot_sb[:], scr_ap, fam, reads=[scr_buf],
                    writes=[slot_buf])

        ring = {"A": 0, "B": 0}

        def nextA():
            i = ring["A"] % 2
            ring["A"] += 1
            return wA[i], b_wA[i], f"wA{i}"

        def nextB():
            i = ring["B"] % 2
            ring["B"] += 1
            return wB[i], b_wB[i], f"wB{i}"

        pmrot = [0]

        def next_pm():
            i = pmrot[0] % 6
            pmrot[0] += 1
            return i

        for dst, src in ((bupc, bupc_d), (bswc, bswc_d), (bsac, bsac_d),
                         (bsuc, bsuc_d), (ones_s, ones_d), (sel8, sel8_d),
                         (selg, selg_d), (ident, ident_d)):
            dma("Y", dst[:], src, "cst", writes=[b_const], acc=True)
        for dst, src in ((bdn2d, bdn_d), (bsd_r, bsd_d), (bg2d, bg_d),
                         (wg_sb, wg_d)):
            dma("G", dst[:], src, "cst", writes=[b_const], acc=True)

        def ln_tile(t, src_chunk_dma, rstd_ap, negmr_ap, b_rstd, rstd_acc):
            for oc in range(4):
                sl = oc
                src_chunk_dma(oc, sl)
                P.emit("V", lambda eng, t=t, oc=oc, sl=sl: eng.bn_stats(
                    stats[:, t, oc, :], xst[:, sl, :]),
                    reads=[b_xst[sl]], writes=[b_stats[t]], acc=(oc > 0))
            P.emit("V", lambda eng, t=t: eng.bn_aggr(
                mv[:], stats[:, t, :, :].rearrange("p a b -> p (a b)")),
                reads=[b_stats[t]], writes=[b_mv])
            P.emit("V", lambda eng: eng.tensor_scalar(
                lncols[:, 0:1], mv[:, 1:2], EPS, None, ALU.add),
                reads=[b_mv], writes=[b_lncols])
            P.emit("S", lambda eng: eng.activation(
                lncols[:, 3:4], lncols[:, 0:1], AF.Ln),
                reads=[b_lncols], writes=[b_lncols], acc=True)
            P.emit("S", lambda eng, r=rstd_ap: eng.activation(
                r, lncols[:, 3:4], AF.Exp, scale=-0.5),
                reads=[b_lncols], writes=[b_rstd], acc=rstd_acc)
            P.emit("V", lambda eng, r=rstd_ap, n=negmr_ap: eng.tensor_scalar(
                n, mv[:, 0:1], r, -1.0, ALU.mult, ALU.mult),
                reads=[b_mv, b_rstd], writes=[b_rstd], acc=True)

        def emit_supertile(stidx):
            row0 = stidx * ST

            for t in range(NT):
                r0 = row0 + t * 128

                def xload(oc, sl, r0=r0):
                    dma("S", xst[:, sl, :],
                        x_d[r0:r0 + 128, oc * 512:(oc + 1) * 512],
                        f"xld{sl}", writes=[b_xst[sl]])

                ln_tile(t, xload, lncols[:, 1:2], lncols[:, 2:3], b_lncols, True)
                for oc in range(4):
                    P.emit("S", lambda eng, oc=oc: eng.activation(
                        lnbf[:, oc * 512:(oc + 1) * 512], xst[:, oc, :],
                        AF.Identity, bias=lncols[:, 2:3],
                        scale=lncols[:, 1:2]),
                        reads=[b_xst[oc], b_lncols], writes=[b_lnbf],
                        acc=(oc > 0))
                for kc in range(16):
                    dma("Y", fmajT[:, kc, t * 128:(t + 1) * 128],
                        lnbf[:, kc * 128:(kc + 1) * 128],
                        rot("tf", 4), reads=[b_lnbf], writes=[b_fmajT],
                        acc=not (t == 0 and kc == 0), transpose=True)

            def attn_scores_sub(t, koc):
                qc = 0 * NT * D + t * D
                kcol = 1 * NT * D + t * D
                for h in range(NH):
                    for k in range(4 * koc, 4 * koc + 4):
                        P.emit("V", lambda eng, h=h, k=k, qc=qc, kcol=kcol,
                               t=t: eng.scalar_tensor_tensor(
                                   scr_v[:],
                                   qkv[:, qc + h * 128:qc + (h + 1) * 128],
                                   1.0,
                                   qkv[:, kcol + k * 128:kcol + (k + 1) * 128],
                                   ALU.mult, ALU.mult,
                                   accum_out=scoresb[:, t, h * 16 + k:
                                                     h * 16 + k + 1]),
                               reads=[b_qkv[0][t], b_qkv[1][t]],
                               writes=[b_scores[t]], noself=True,
                               acc=not (h == 0 and k == 4 * koc and koc == 0))

            def attn_o(t):
                vc = 2 * NT * D + t * D
                for h in range(NH):
                    osl = o32[:, h * 128:(h + 1) * 128]
                    for k in range(NH):
                        col = scoresb[:, t, h * 16 + k:h * 16 + k + 1]
                        vsl = qkv[:, vc + k * 128:vc + (k + 1) * 128]
                        if k == 0:
                            P.emit("V", lambda eng, c=col, v=vsl, o=osl:
                                   eng.tensor_scalar(o, v, c, None, ALU.mult),
                                   reads=[b_scores[t], b_qkv[2][t]],
                                   writes=[b_o32], noself=True,
                                   acc=(h > 0))
                        else:
                            P.emit("V", lambda eng, c=col, v=vsl, o=osl:
                                   eng.scalar_tensor_tensor(
                                       o, v, c, o, ALU.mult, ALU.add),
                                   reads=[b_scores[t], b_qkv[2][t], b_o32],
                                   writes=[b_o32], noself=True, acc=True)

            def attn_proj(wi, wname, wd, sub=None):
                for oc in range(4):
                    slot, sb, fam = nextA()
                    load_w(stidx, slot, sb, fam, wd[oc], scr[wname][oc],
                           b_scr[wname])
                    for t in range(NT):
                        pi = next_pm()
                        for kc in range(16):
                            mm(pm[pi][:], fmajT[:, kc, t * 128:(t + 1) * 128],
                               slot[:, kc, :], kc == 0, kc == 15,
                               [b_fmajT, sb], b_pm[pi])
                        c0 = wi * NT * D + t * D + oc * 512
                        P.emit("S", lambda eng, pi=pi, c0=c0: eng.copy(
                            qkv[:, c0:c0 + 512], pm[pi][:]),
                            reads=[b_pm[pi]], writes=[b_qkv[wi][t]],
                            hazards=al_w[wi], acc=(oc > 0))
                        if sub is not None:
                            sub(t, oc)

            def attn_fin(t):
                P.emit("S", lambda eng: eng.copy(obf[:], o32[:]),
                       reads=[b_o32], writes=[b_obf])
                for kc in range(16):
                    dma("Y", othsu[:, kc, t * 128:(t + 1) * 128],
                        obf[:, kc * 128:(kc + 1) * 128],
                        rot("to", 4), reads=[b_obf], writes=[b_oT[t]],
                        hazards=al_oT, acc=(kc > 0), transpose=True)

            attn_proj(0, "wq", wq_d)
            attn_proj(1, "wk", wk_d, sub=attn_scores_sub)
            attn_proj(2, "wv", wv_d)
            for t in range(NT):
                attn_o(t)
                attn_fin(t)

            for oc in range(4):
                slot, sb, fam = nextA()
                load_w(stidx, slot, sb, fam, wo_d[oc], scr["wo"][oc],
                       b_scr["wo"])
                for t in range(NT):
                    r0 = row0 + t * 128
                    pi = next_pm()
                    for kc in range(16):
                        mm(pm[pi][:], othsu[:, kc, t * 128:(t + 1) * 128],
                           slot[:, kc, :], kc == 0, kc == 15,
                           [b_oT[t], sb], b_pm[pi])
                    sl = t
                    dma("S", xst[:, sl, :],
                        x_d[r0:r0 + 128, oc * 512:(oc + 1) * 512],
                        f"xld{sl}", writes=[b_xst[sl]])
                    P.emit("V", lambda eng, pi=pi, sl=sl: eng.tensor_add(
                        x2st[:, sl, :], xst[:, sl, :], pm[pi][:]),
                        reads=[b_xst[sl], b_pm[pi]], writes=[b_x2st[sl]])
                    P.emit("V", lambda eng, t=t, oc=oc, sl=sl: eng.bn_stats(
                        stats[:, t, oc, :], x2st[:, sl, :]),
                        reads=[b_x2st[sl]], writes=[b_stats[t]], acc=(oc > 0))
                    dma("S", x2scr_d[r0:r0 + 128, oc * 512:(oc + 1) * 512],
                        x2st[:, sl, :], rot("sx", 2), reads=[b_x2st[sl]],
                        writes=[b_x2scr], acc=not (oc == 0 and t == 0))

            for t in range(NT):
                r0 = row0 + t * 128
                P.emit("V", lambda eng, t=t: eng.bn_aggr(
                    mv[:], stats[:, t, :, :].rearrange("p a b -> p (a b)")),
                    reads=[b_stats[t]], writes=[b_mv])
                P.emit("V", lambda eng: eng.tensor_scalar(
                    lncols[:, 0:1], mv[:, 1:2], EPS, None, ALU.add),
                    reads=[b_mv], writes=[b_lncols])
                P.emit("S", lambda eng: eng.activation(
                    lncols[:, 3:4], lncols[:, 0:1], AF.Ln),
                    reads=[b_lncols], writes=[b_lncols], acc=True)
                P.emit("S", lambda eng, t=t: eng.activation(
                    m2r2[:, t, 0:1], lncols[:, 3:4], AF.Exp, scale=-0.5),
                    reads=[b_lncols], writes=[b_m2r2], acc=(t > 0))
                P.emit("V", lambda eng, t=t: eng.tensor_scalar(
                    m2r2[:, t, 1:2], mv[:, 0:1], m2r2[:, t, 0:1], -1.0,
                    ALU.mult, ALU.mult),
                    reads=[b_mv, b_m2r2], writes=[b_m2r2], acc=True)
                for oc in range(4):
                    sl = oc
                    dma("S", x2st[:, sl, :],
                        x2scr_d[r0:r0 + 128, oc * 512:(oc + 1) * 512],
                        f"x2ld{sl}", reads=[b_x2scr], writes=[b_x2st[sl]])
                    P.emit("S", lambda eng, t=t, oc=oc, sl=sl: eng.activation(
                        lnbf[:, oc * 512:(oc + 1) * 512], x2st[:, sl, :],
                        AF.Identity, bias=m2r2[:, t, 1:2],
                        scale=m2r2[:, t, 0:1]),
                        reads=[b_x2st[sl], b_m2r2], writes=[b_lnbf],
                        acc=(oc > 0))
                for kc in range(16):
                    dma("Y", fmajT[:, kc, t * 128:(t + 1) * 128],
                        lnbf[:, kc * 128:(kc + 1) * 128],
                        rot("tf", 4), reads=[b_lnbf], writes=[b_fmajT],
                        acc=not (t == 0 and kc == 0), transpose=True)

            for j in range(2):
                for t in range(NT):
                    for kc in range(16):
                        mm(pm[6][:, 0:8], fmajT[:, kc, t * 128:(t + 1) * 128],
                           wg_sb[:, kc, j * 8:(j + 1) * 8], kc == 0, False,
                           [b_fmajT, b_const], b_pm[6])
                    mm(pm[6][:, 0:8], selg[0:2, j * 128:(j + 1) * 128],
                       bg2d[0:2, :], False, True, [b_const], b_pm[6])
                    P.emit("S", lambda eng, t=t: eng.copy(
                        g8[:, t, :], pm[6][:, 0:8]),
                        reads=[b_pm[6]], writes=[b_g8[t]])
                    gap = g8[:, t, :]
                    P.emit("V", lambda eng, g=gap: eng.tensor_reduce(
                        rcol[:, 0:1], g, AX.X, ALU.min),
                        reads=[b_g8[t]], writes=[b_rcol])
                    P.emit("V", lambda eng, g=gap: eng.tensor_scalar(
                        rtmp[:, 0, :], g, rcol[:, 0:1], None, ALU.is_equal),
                        reads=[b_g8[t], b_rcol], writes=[b_rtmp])
                    P.emit("V", lambda eng, g=gap: eng.scalar_tensor_tensor(
                        rtmp[:, 1, :], rtmp[:, 0, :], 1e30, g,
                        ALU.mult, ALU.add),
                        reads=[b_g8[t], b_rtmp], writes=[b_rtmp], acc=True)
                    P.emit("V", lambda eng: eng.tensor_reduce(
                        rcol[:, 1:2], rtmp[:, 1, :], AX.X, ALU.min),
                        reads=[b_rtmp], writes=[b_rcol], acc=True)
                    P.emit("V", lambda eng, g=gap: eng.tensor_scalar(
                        rtmp[:, 1, :], g, rcol[:, 1:2], None, ALU.is_gt),
                        reads=[b_g8[t], b_rcol], writes=[b_rtmp], acc=True)
                    P.emit("V", lambda eng, g=gap: eng.reduce_max(
                        rcol[:, 2:3], g, AX.X, negate=True),
                        reads=[b_g8[t]], writes=[b_rcol], acc=True)
                    P.emit("S", lambda eng, g=gap: eng.activation(
                        rtmp[:, 2, :], g, AF.Exp, bias=rcol[:, 2:3]),
                        reads=[b_g8[t], b_rcol], writes=[b_rtmp], acc=True)
                    P.emit("V", lambda eng: eng.tensor_mul(
                        rtmp[:, 0, :], rtmp[:, 2, :], rtmp[:, 1, :]),
                        reads=[b_rtmp], writes=[b_rtmp], acc=True)
                    P.emit("V", lambda eng: eng.tensor_reduce(
                        rcol[:, 3:4], rtmp[:, 0, :], AX.X, ALU.add),
                        reads=[b_rtmp], writes=[b_rcol], acc=True)
                    P.emit("V", lambda eng: eng.reciprocal(
                        rcol[:, 4:5], rcol[:, 3:4]),
                        reads=[b_rcol], writes=[b_rcol], acc=True)
                    P.emit("V", lambda eng, t=t: eng.tensor_scalar(
                        w_r[:, t, :], rtmp[:, 0, :], rcol[:, 4:5], None,
                        ALU.mult),
                        reads=[b_rtmp, b_rcol], writes=[b_w[t]])
                    P.emit("S", lambda eng, t=t: eng.copy(wbf[:], w_r[:, t, :]),
                           reads=[b_w[t]], writes=[b_wbf])
                    P.emit("T", lambda eng: eng.transpose(
                        ptp[0:8, 0:128], wbf[:], ident[:]),
                        reads=[b_wbf, b_const], writes=[b_ptp])
                    P.emit("S", lambda eng, t=t: eng.copy(
                        wT[0:8, t * 128:(t + 1) * 128], ptp[0:8, 0:128]),
                        reads=[b_ptp], writes=[b_wT], acc=(t > 0))

                for ex in range(6):
                    bws = ex % 2
                    mm(pm[6][:, 0:512], sel8[0:8, ex * 128:(ex + 1) * 128],
                       wT[0:8, :], True, True, [b_const, b_wT], b_pm[6])
                    P.emit("S", lambda eng, s=bws: eng.copy(
                        bw_sb[s][:], pm[6][:, 0:512]),
                        reads=[b_pm[6]], writes=[b_bw[bws]])
                    for och in range(2):
                        slot, sb, fam = nextA()
                        load_w(stidx, slot, sb, fam, wup_d[ex, och],
                               scr["wup"][ex, och], b_scr["wup"])
                        for ofc in range(4):
                            pi = next_pm()
                            for kc in range(16):
                                mm(pm[pi][:],
                                   slot[:, kc, ofc * 128:(ofc + 1) * 128],
                                   fmajT[:, kc, :], kc == 0, kc == 15,
                                   [b_fmajT, sb], b_pm[pi])
                            cc = och * 4 + ofc
                            P.emit("V", lambda eng, pi=pi, cc=cc, ex=ex:
                                   eng.tensor_scalar(
                                       othsu[:, cc, :], pm[pi][:],
                                       bupc[:, ex * 8 + cc:ex * 8 + cc + 1],
                                       None, ALU.add),
                                   reads=[b_pm[pi], b_const],
                                   writes=[b_hF], hazards=al_h,
                                   acc=(och * 4 + ofc > 0))
                    for och in range(2):
                        slot, sb, fam = nextB()
                        load_w(stidx, slot, sb, fam, wsw_d[ex, och],
                               scr["wsw"][ex, och], b_scr["wsw"])
                        for ofc in range(4):
                            pi = next_pm()
                            for kc in range(8):
                                mm(pm[pi][:],
                                   slot[:, kc, ofc * 128:(ofc + 1) * 128],
                                   othsu[:, kc, :], kc == 0, kc == 7,
                                   [b_hF, sb], b_pm[pi])
                            cc = och * 4 + ofc
                            P.emit("S", lambda eng, pi=pi, cc=cc, ex=ex:
                                   eng.activation(
                                       othsu[:, 8 + cc, :], pm[pi][:],
                                       AF.Silu,
                                       bias=bswc[:, ex * 8 + cc:
                                                 ex * 8 + cc + 1]),
                                   reads=[b_pm[pi], b_const],
                                   writes=[b_suF], hazards=al_su,
                                   acc=(och * 4 + ofc > 0))
                    gc0 = ex * 4096
                    for cc in range(8):
                        P.emit("V", lambda eng, cc=cc: eng.tensor_mul(
                            othsu[:, 8 + cc, :], othsu[:, 8 + cc, :],
                            othsu[:, cc, :]),
                            reads=[b_hF, b_suF], writes=[b_suF], acc=True,
                            noself=True)
                        P.emit("V", lambda eng, cc=cc, gc0=gc0, s=bws:
                               eng.tensor_mul(
                                   qkv[:, gc0 + cc * 512:gc0 + (cc + 1) * 512],
                                   othsu[:, 8 + cc, :], bw_sb[s][:]),
                               reads=[b_suF, b_bw[bws]], writes=[b_gf[ex]],
                               hazards=al_gf[ex], acc=(cc > 0), noself=True)

                for c in range(2):
                    for q2 in range(2):
                        slot, sb, fam = nextA()
                        ocu = c * 2 + q2
                        load_w(stidx, slot, sb, fam, wsu_d[ocu],
                               scr["wsu"][ocu], b_scr["wsu"])
                        for ofc in range(4):
                            pi = next_pm()
                            for kc in range(16):
                                mm(pm[pi][:],
                                   slot[:, kc, ofc * 128:(ofc + 1) * 128],
                                   fmajT[:, kc, :], kc == 0, kc == 15,
                                   [b_fmajT, sb], b_pm[pi])
                            cc = q2 * 4 + ofc
                            P.emit("V", lambda eng, pi=pi, cc=cc, c=c:
                                   eng.tensor_scalar(
                                       shuT[:, cc, :], pm[pi][:],
                                       bsuc[:, c * 8 + cc:c * 8 + cc + 1],
                                       None, ALU.add),
                                   reads=[b_pm[pi], b_const], writes=[b_shuT],
                                   acc=(q2 * 4 + ofc > 0))
                    for och in range(2):
                        slot, sb, fam = nextB()
                        load_w(stidx, slot, sb, fam, wsa_d[och],
                               scr["wsa"][och], b_scr["wsa"])
                        for ofc in range(4):
                            pi = next_pm()
                            for kc in range(8):
                                mm(pm[pi][:],
                                   slot[:, kc, ofc * 128:(ofc + 1) * 128],
                                   shuT[:, kc, :], kc == 0, kc == 7,
                                   [b_shuT, sb], b_pm[pi])
                            cc = och * 4 + ofc
                            P.emit("S", lambda eng, pi=pi, cc=cc:
                                   eng.activation(
                                       othsu[:, cc, :], pm[pi][:], AF.Silu,
                                       bias=bsac[:, cc:cc + 1]),
                                   reads=[b_pm[pi], b_const], writes=[b_hF],
                                   hazards=al_h, acc=(och * 4 + ofc > 0))
                    for cc in range(8):
                        P.emit("V", lambda eng, cc=cc, c=c: eng.tensor_mul(
                            swF[:, c * 8 + cc, :], othsu[:, cc, :],
                            shuT[:, cc, :]),
                            reads=[b_hF, b_shuT], writes=[b_swF],
                            acc=not (c == 0 and cc == 0))

                for oc in range(4):
                    pis = [0, 1, 2, 3] if oc % 2 == 0 else [4, 5, 6, 0]
                    for t in range(NT):
                        r0 = row0 + t * 128
                        dma("S", x2st[:, t, :],
                            x2scr_d[r0:r0 + 128, oc * 512:(oc + 1) * 512],
                            f"x2ld{t}", reads=[b_x2scr], writes=[b_x2st[t]])
                    for ei in range(6):
                        slot, sb, fam = nextB()
                        load_w(stidx, slot, sb, fam, wdn_d[ei, oc],
                               scr["wdn"][ei, oc], b_scr["wdn"])
                        gv = gf_view(ei)
                        for t in range(NT):
                            for kc in range(8):
                                mm(pm[pis[t]][:],
                                   gv[:, kc, t * 128:(t + 1) * 128],
                                   slot[:, kc, :], ei == 0 and kc == 0, False,
                                   [b_gf[ei], sb], b_pm[pis[t]])
                    slot, sb, fam = nextA()
                    load_w(stidx, slot, sb, fam, wsd_d[oc], scr["wsd"][oc],
                           b_scr["wsd"])
                    for t in range(NT):
                        for kc in range(16):
                            mm(pm[pis[t]][:],
                               swF[:, kc, t * 128:(t + 1) * 128],
                               slot[:, kc, :], False, False,
                               [b_swF, sb], b_pm[pis[t]])
                    for t in range(NT):
                        mm(pm[pis[t]][:], wT[0:6, t * 128:(t + 1) * 128],
                           bdn2d[0:6, oc * 512:(oc + 1) * 512], False, False,
                           [b_wT, b_const], b_pm[pis[t]])
                        mm(pm[pis[t]][:], ones_s[0:1, :],
                           bsd_r[0:1, oc * 512:(oc + 1) * 512], False, True,
                           [b_const], b_pm[pis[t]])
                    for t in range(NT):
                        r0 = row0 + t * 128
                        pi = pis[t]
                        if j == 0:
                            sl = t % 2
                            P.emit("S", lambda eng, t=t, sl=sl: eng.activation(
                                xt1bf[:, sl, :], x2st[:, t, :], AF.Identity,
                                bias=m2r2[:, t, 1:2], scale=m2r2[:, t, 0:1]),
                                reads=[b_x2st[t], b_m2r2],
                                writes=[b_xt1bf[sl]])
                            P.emit("V", lambda eng, pi=pi, sl=sl:
                                   eng.tensor_add(
                                       xt1bf[:, sl, :], xt1bf[:, sl, :],
                                       pm[pi][:]),
                                   reads=[b_pm[pi], b_xt1bf[sl]],
                                   writes=[b_xt1bf[sl]], acc=True)
                            for kq in range(4):
                                dma("Y", fmajT[:, oc * 4 + kq,
                                               t * 128:(t + 1) * 128],
                                    xt1bf[:, sl, kq * 128:(kq + 1) * 128],
                                    rot("tf", 4),
                                    reads=[b_xt1bf[sl]], writes=[b_fmajT],
                                    acc=not (oc == 0 and t == 0 and kq == 0),
                                    transpose=True)
                        else:
                            sl = t % 2
                            P.emit("S", lambda eng, t=t, sl=sl: eng.activation(
                                outst[:, sl, :], x2st[:, t, :], AF.Identity,
                                bias=m2r2[:, t, 1:2], scale=m2r2[:, t, 0:1]),
                                reads=[b_x2st[t], b_m2r2],
                                writes=[b_outst[sl]])
                            P.emit("V", lambda eng, pi=pi, sl=sl:
                                   eng.tensor_add(
                                       outst[:, sl, :], outst[:, sl, :],
                                       pm[pi][:]),
                                   reads=[b_pm[pi], b_outst[sl]],
                                   writes=[b_outst[sl]], acc=True)
                            P.emit("V", lambda eng, t=t, sl=sl: eng.tensor_add(
                                outst[:, sl, :], outst[:, sl, :],
                                x2st[:, t, :]),
                                reads=[b_x2st[t], b_outst[sl]],
                                writes=[b_outst[sl]], acc=True)
                            dma("S", y_d[r0:r0 + 128, oc * 512:(oc + 1) * 512],
                                outst[:, sl, :], rot("sy", 2),
                                reads=[b_outst[sl]])

        for stidx in range(nst):
            emit_supertile(stidx)

        P.barrier("S", "sy0", P.cnt["sy0"])
        P.barrier("S", "sy1", P.cnt["sy1"])
        P.barrier("Y", "sy0", P.cnt["sy0"])
        P.barrier("Y", "sy1", P.cnt["sy1"])

        with nc.Block() as block:
            @block.tensor
            def _(eng):
                P.replay("T", eng)

            @block.vector
            def _(eng):
                P.replay("V", eng)

            @block.scalar
            def _(eng):
                P.replay("S", eng)

            @block.sync
            def _(eng):
                P.replay("Y", eng)

            @block.gpsimd
            def _(eng):
                P.replay("G", eng)

    return nc



def _chunkK(w):
    K, F = w.shape
    return np.ascontiguousarray(
        w.reshape(K // 128, 128, F // 512, 512).transpose(2, 1, 0, 3))


def _prep_inputs(inp):
    f32 = lambda a: np.asarray(a, dtype=np.float32)
    wq = f32(inp["L0_attn_Wq"]); wk = f32(inp["L0_attn_Wk"])
    wv = f32(inp["L0_attn_Wv"]); wo = f32(inp["L0_attn_Wo"])
    wg = f32(inp["L0_ffn_Wg"]); bg = f32(inp["L0_ffn_bg"])
    wup = f32(inp["L0_ffn_Wup"]); bup = f32(inp["L0_ffn_bup"])
    wsw = f32(inp["L0_ffn_Wsw"]); bsw = f32(inp["L0_ffn_bsw"])
    wdn = f32(inp["L0_ffn_Wdn"]); bdn = f32(inp["L0_ffn_bdn"])
    wsu = f32(inp["L0_ffn_Wsu"]); bsu = f32(inp["L0_ffn_bsu"])
    wsa = f32(inp["L0_ffn_Wsa"]); bsa = f32(inp["L0_ffn_bsa"])
    wsd = f32(inp["L0_ffn_Wsd"]); bsd = f32(inp["L0_ffn_bsd"])

    bf = ml_dtypes.bfloat16
    sel8 = np.zeros((8, 1024), dtype=bf)
    for ei in range(8):
        sel8[ei, ei * 128:(ei + 1) * 128] = 1
    selg = np.zeros((2, 256), dtype=bf)
    for ji in range(2):
        selg[ji, ji * 128:(ji + 1) * 128] = 1

    shared = {
        "wq_c": _chunkK(wq), "wk_c": _chunkK(wk), "wv_c": _chunkK(wv),
        "wo_c": _chunkK(wo),
        "wup_c": np.stack([_chunkK(wup[i]) for i in range(6)]),
        "wsw_c": np.stack([_chunkK(wsw[i]) for i in range(6)]),
        "wdn_c": np.stack([_chunkK(wdn[i]) for i in range(6)]),
        "wsu_c": _chunkK(wsu), "wsa_c": _chunkK(wsa), "wsd_c": _chunkK(wsd),
        "wg_c": np.ascontiguousarray(
            wg.reshape(2, 16, 128, 8).transpose(2, 1, 0, 3).reshape(128, 16, 16)),
        "bupc": np.ascontiguousarray(
            bup.reshape(6, 8, 128).transpose(2, 0, 1).reshape(128, 48)),
        "bswc": np.ascontiguousarray(
            bsw.reshape(6, 8, 128).transpose(2, 0, 1).reshape(128, 48)),
        "bsac": np.ascontiguousarray(bsa.reshape(8, 128).T),
        "bsuc": np.ascontiguousarray(bsu.reshape(16, 128).T),
        "bdn2d": np.ascontiguousarray(bdn),
        "bsd_r": np.ascontiguousarray(bsd[None, :]),
        "bg2d": np.ascontiguousarray(bg),
        "ones_c": np.ones((1, 128), dtype=bf),
        "sel8_c": sel8, "selg_c": selg,
        "ident_c": np.eye(128, dtype=bf),
    }
    return shared


def kernel(**inputs):
    x = np.asarray(inputs["L0_attn_x"], dtype=np.float32)
    B, N, _ = x.shape
    xf = np.ascontiguousarray(x.reshape(B * N, D))
    shared = _prep_inputs(inputs)
    nc = build_program(TPC)
    in_maps = [dict(shared, x=np.ascontiguousarray(xf[c * TPC:(c + 1) * TPC]))
               for c in range(N_CORES)]
    res = run_bass_kernel_spmd(nc, in_maps, core_ids=list(range(N_CORES)))
    y = np.concatenate([r["y"] for r in res.results], axis=0)
    return y.reshape(B, N, D)


# version 08
# speedup vs baseline: 1.2069x; 1.0989x over previous
import numpy as np
import ml_dtypes
import concourse.bass as bass
import concourse.mybir as mybir
from concourse.bass_utils import run_bass_kernel_spmd

F32 = mybir.dt.float32
BF16 = mybir.dt.bfloat16
AF = mybir.ActivationFunctionType
ALU = mybir.AluOpType
AX = mybir.AxisListType

N_CORES = 8
TPC = 4096
ST = 512
NT = 4
D = 2048
NH = 16
EPS = 1e-5



class Buf:
    __slots__ = ("name", "writers", "readers")

    def __init__(self, name):
        self.name = name
        self.writers = {}
        self.readers = {}


class Ins:
    __slots__ = ("waits", "build", "incs")

    def __init__(self, waits, build, incs):
        self.waits, self.build, self.incs = waits, build, incs


class Prog:
    STREAMS = ("T", "V", "S", "Y", "G")

    def __init__(self, nc, sems):
        self.nc = nc
        self.sems = sems
        self.cnt = {k: 0 for k in sems}
        self.ir = {s: [] for s in self.STREAMS}
        self.waited = {s: {} for s in self.STREAMS}

    def emit(self, stream, build, reads=(), writes=(), hazards=(), dma=False,
             acc=False, dsem=None, noself=False):
        if dma:
            sem = dsem
            inc = 16
        else:
            sem = stream
            inc = 1
        waits = {}
        if dma and self.cnt[sem] > 0 and \
                self.waited[stream].get(sem, -1) < self.cnt[sem]:
            waits[sem] = self.cnt[sem]

        def need(dep):
            for s, v in dep.items():
                if stream == "T" and s == "T" and not dma:
                    continue
                if noself and s == stream and not dma:
                    continue
                if self.waited[stream].get(s, -1) >= v:
                    continue
                if waits.get(s, -1) < v:
                    waits[s] = v

        for b in reads:
            need(b.writers)
        if not acc:
            for b in writes:
                need(b.writers)
                need(b.readers)
            for b in hazards:
                need(b.writers)
                need(b.readers)
        nv = self.cnt[sem] + inc
        self.cnt[sem] = nv
        for s, v in waits.items():
            assert v <= self.cnt[s], f"wait on future {s}:{v}>{self.cnt[s]}"
            if self.waited[stream].get(s, -1) < v:
                self.waited[stream][s] = v
        for b in reads:
            if b.readers.get(sem, -1) < nv:
                b.readers[sem] = nv
        for b in writes:
            if acc:
                b.writers[sem] = max(b.writers.get(sem, -1), nv)
            else:
                b.writers = {sem: nv}
                b.readers = {}
        self.ir[stream].append(
            Ins([(self.sems[s], v) for s, v in sorted(waits.items())],
                build, [(self.sems[sem], inc)]))

    def barrier(self, stream, sem_name, value):
        self.ir[stream].append(Ins([(self.sems[sem_name], value)], None, []))

    def replay(self, stream, eng):
        for ins in self.ir[stream]:
            for sem, v in ins.waits:
                eng.wait_ge(sem, v)
            if ins.build is not None:
                bi = ins.build(eng)
                for sem, d in ins.incs:
                    bi.then_inc(sem, d)



def build_program(n_tok, dbg_x2=False):
    nst = n_tok // ST
    nc = bass.Bass("TRN2", target_bir_lowering=False)

    x_d = nc.dram_tensor("x", [n_tok, D], F32, kind="ExternalInput").ap()
    y_d = nc.dram_tensor("y", [n_tok, D], F32, kind="ExternalOutput").ap()

    def din(name, shape, dt=F32):
        return nc.dram_tensor(name, shape, dt, kind="ExternalInput").ap()

    wq_d = din("wq_c", [4, 128, 16, 512])
    wk_d = din("wk_c", [4, 128, 16, 512])
    wv_d = din("wv_c", [4, 128, 16, 512])
    wo_d = din("wo_c", [4, 128, 16, 512])
    wup_d = din("wup_c", [6, 2, 128, 16, 512])
    wsw_d = din("wsw_c", [6, 2, 128, 8, 512])
    wdn_d = din("wdn_c", [6, 4, 128, 8, 512])
    wsu_d = din("wsu_c", [4, 128, 16, 512])
    wsa_d = din("wsa_c", [2, 128, 8, 512])
    wsd_d = din("wsd_c", [4, 128, 16, 512])
    wg_d = din("wg_c", [128, 16, 16])
    bupc_d = din("bupc", [128, 48])
    bswc_d = din("bswc", [128, 48])
    bsac_d = din("bsac", [128, 8])
    bsuc_d = din("bsuc", [128, 16])
    bdn_d = din("bdn2d", [6, D])
    bsd_d = din("bsd_r", [1, D])
    bg_d = din("bg2d", [2, 8])
    ones_d = din("ones_c", [1, 128], BF16)
    sel8_d = din("sel8_c", [8, 1024], BF16)
    selg_d = din("selg_c", [2, 256], BF16)
    ident_d = din("ident_c", [128, 128], BF16)

    x2scr_d = nc.dram_tensor("x2scr", [n_tok, D], F32,
                             kind="ExternalOutput" if dbg_x2 else "Internal").ap()
    scr = {}
    for nm, shp in (("wq", [4, 128, 16, 512]), ("wk", [4, 128, 16, 512]),
                    ("wv", [4, 128, 16, 512]), ("wo", [4, 128, 16, 512]),
                    ("wup", [6, 2, 128, 16, 512]), ("wsw", [6, 2, 128, 8, 512]),
                    ("wdn", [6, 4, 128, 8, 512]), ("wsu", [4, 128, 16, 512]),
                    ("wsa", [2, 128, 8, 512]), ("wsd", [4, 128, 16, 512])):
        scr[nm] = nc.dram_tensor("scr_" + nm, shp, BF16, kind="Internal").ap()

    from contextlib import ExitStack
    with ExitStack() as ctx:
        e = ctx.enter_context

        fmajT = e(nc.sbuf_tensor("fmajT", [128, 16, ST], BF16))
        qkv = e(nc.sbuf_tensor("qkv", [128, 3 * NT * D], BF16))
        othsu = e(nc.sbuf_tensor("othsu", [128, 16, ST], BF16))
        o32 = e(nc.sbuf_tensor("o32", [128, D], F32))
        lnbf = e(nc.sbuf_tensor("lnbf", [128, D], BF16))
        obf = lnbf
        scoresb = e(nc.sbuf_tensor("scoresb", [128, NT, 256], F32))
        scr_v = e(nc.sbuf_tensor("scr_v", [128, 128], BF16))
        shuT = e(nc.sbuf_tensor("shuT", [128, 8, ST], BF16))
        swF = e(nc.sbuf_tensor("swF", [128, 16, ST], BF16))
        xst = e(nc.sbuf_tensor("xst", [128, 4, 512], F32))
        x2st = e(nc.sbuf_tensor("x2st", [128, 4, 512], F32))
        outst = e(nc.sbuf_tensor("outst", [128, 2, 512], F32))
        xt1bf = e(nc.sbuf_tensor("xt1bf", [128, 2, 512], BF16))
        wA = [e(nc.sbuf_tensor(f"wA{i}", [128, 16, 512], BF16)) for i in range(2)]
        wB = [e(nc.sbuf_tensor(f"wB{i}", [128, 8, 512], BF16)) for i in range(2)]
        bw_sb = [e(nc.sbuf_tensor(f"bw{i}", [128, 512], BF16)) for i in range(2)]
        wg_sb = e(nc.sbuf_tensor("wg_sb", [128, 16, 16], BF16))
        bupc = e(nc.sbuf_tensor("bupc_s", [128, 48], F32))
        bswc = e(nc.sbuf_tensor("bswc_s", [128, 48], F32))
        bsac = e(nc.sbuf_tensor("bsac_s", [128, 8], F32))
        bsuc = e(nc.sbuf_tensor("bsuc_s", [128, 16], F32))
        bdn2d = e(nc.sbuf_tensor("bdn2d_s", [6, D], BF16))
        bsd_r = e(nc.sbuf_tensor("bsd_s", [1, D], BF16))
        bg2d = e(nc.sbuf_tensor("bg2d_s", [2, 8], BF16))
        ones_s = e(nc.sbuf_tensor("ones_s", [1, 128], BF16))
        sel8 = e(nc.sbuf_tensor("sel8_s", [8, 1024], BF16))
        selg = e(nc.sbuf_tensor("selg_s", [2, 256], BF16))
        ident = e(nc.sbuf_tensor("ident_s", [128, 128], BF16))
        stats = e(nc.sbuf_tensor("stats", [128, NT, 4, 6], F32))
        mv = e(nc.sbuf_tensor("mv", [128, 2], F32))
        lncols = e(nc.sbuf_tensor("lncols", [128, 4], F32))
        m2r2 = e(nc.sbuf_tensor("m2r2", [128, NT, 2], F32))
        g8 = e(nc.sbuf_tensor("g8", [128, NT, 8], F32))
        rtmp = e(nc.sbuf_tensor("rtmp", [128, 3, 8], F32))
        rcol = e(nc.sbuf_tensor("rcol", [128, 8], F32))
        w_r = e(nc.sbuf_tensor("w_r", [128, NT, 8], F32))
        wbf = e(nc.sbuf_tensor("wbf", [128, 8], BF16))
        wT = e(nc.sbuf_tensor("wT", [8, ST], BF16))

        pm = [e(nc.psum_tensor(f"pm{i}", [128, 512], F32)) for i in range(7)]
        ptp = e(nc.psum_tensor("ptp", [128, 1024], BF16))

        dma_fams = (["cst", "xld0", "xld1", "xld2", "xld3",
                     "x2ld0", "x2ld1", "x2ld2", "x2ld3",
                     "wA0", "wA1", "wB0", "wB1",
                     "scst0", "scst1", "scst2",
                     "sx0", "sx1", "sy0", "sy1"]
                    + [f"tf{i}" for i in range(4)]
                    + [f"to{i}" for i in range(4)])
        sems = {k: e(nc.semaphore(f"sem_{k}"))
                for k in ["T", "V", "S", "G"] + dma_fams}
        P = Prog(nc, sems)

        b_fmajT = Buf("fmajT")
        b_qkv = [[Buf(f"qkv{w}_{t}") for t in range(NT)] for w in range(3)]
        b_gf = [Buf(f"gF{i}") for i in range(6)]
        b_oT = [Buf(f"oT{t}") for t in range(NT)]
        b_hF, b_suF = Buf("hF"), Buf("suF")
        b_o32, b_lnbf = Buf("o32"), Buf("lnbf")
        b_obf = b_lnbf
        b_scores = [Buf(f"sc{t}") for t in range(NT)]
        b_shuT, b_swF = Buf("shuT"), Buf("swF")
        b_xst = [Buf(f"xst{i}") for i in range(4)]
        b_x2st = [Buf(f"x2st{i}") for i in range(4)]
        b_outst = [Buf("out0"), Buf("out1")]
        b_xt1bf = [Buf("xt1b0"), Buf("xt1b1")]
        b_wA = [Buf("wA0"), Buf("wA1")]
        b_wB = [Buf("wB0"), Buf("wB1")]
        b_bw = [Buf("bw0"), Buf("bw1")]
        b_pm = [Buf(f"pm{i}") for i in range(7)]
        b_ptp = Buf("ptp")
        b_stats = [Buf(f"st{t}") for t in range(NT)]
        b_mv, b_lncols, b_m2r2 = Buf("mv"), Buf("lncols"), Buf("m2r2")
        b_g8 = [Buf(f"g8_{t}") for t in range(NT)]
        b_rtmp, b_rcol = Buf("rtmp"), Buf("rcol")
        b_w = [Buf(f"w{t}") for t in range(NT)]
        b_wbf, b_wT = Buf("wbf"), Buf("wT")
        b_x2scr = Buf("x2scr")
        b_scr = {k: Buf("scr_" + k) for k in scr}
        b_const = Buf("const")

        al_w = {0: [b_gf[0], b_gf[1]], 1: [b_gf[2], b_gf[3]],
                2: [b_gf[4], b_gf[5]]}
        al_gf = {e_: [b_qkv[e_ // 2][2 * (e_ % 2)],
                      b_qkv[e_ // 2][2 * (e_ % 2) + 1]] for e_ in range(6)}
        al_oT = [b_hF, b_suF]
        al_h = list(b_oT)
        al_su = list(b_oT)

        def gf_view(ei):
            return qkv[:, ei * 4096:(ei + 1) * 4096].rearrange(
                "p (k c) -> p k c", c=512)

        rots = {"tf": 0, "to": 0, "scst": 0, "sx": 0, "sy": 0}

        def rot(fam, n):
            i = rots[fam] % n
            rots[fam] += 1
            return f"{fam}{i}"

        def dma(stream, out_ap, in_ap, dsem, reads=(), writes=(), hazards=(),
                acc=False, transpose=False):
            if transpose:
                bld = (lambda eng, o=out_ap, i=in_ap:
                       eng.dma_start_transpose(o, i))
            else:
                bld = (lambda eng, o=out_ap, i=in_ap:
                       eng.dma_start(out=o, in_=i))
            P.emit(stream, bld, reads=reads, writes=writes, hazards=hazards,
                   dma=True, acc=acc, dsem=dsem)

        def mm(psum_ap, lhsT_ap, rhs_ap, start, stop, reads, pbuf):
            bld = (lambda eng, o=psum_ap, l=lhsT_ap, r=rhs_ap, a=start, b=stop:
                   eng.matmul(o, l, r, start=a, stop=b))
            P.emit("T", bld, reads=reads, writes=[pbuf], acc=not start)

        def load_w(stidx, slot_sb, slot_buf, fam, f32_ap, scr_ap, scr_buf):
            if stidx == 0:
                dma("G", slot_sb[:], f32_ap, fam, reads=[b_const],
                    writes=[slot_buf])
                dma("G", scr_ap, slot_sb[:], rot("scst", 3), reads=[slot_buf],
                    writes=[scr_buf], acc=True)
            else:
                dma("G", slot_sb[:], scr_ap, fam, reads=[scr_buf],
                    writes=[slot_buf])

        ring = {"A": 0, "B": 0}

        def nextA():
            i = ring["A"] % 2
            ring["A"] += 1
            return wA[i], b_wA[i], f"wA{i}"

        def nextB():
            i = ring["B"] % 2
            ring["B"] += 1
            return wB[i], b_wB[i], f"wB{i}"

        pmrot = [0]

        def next_pm():
            i = pmrot[0] % 6
            pmrot[0] += 1
            return i

        for dst, src in ((bupc, bupc_d), (bswc, bswc_d), (bsac, bsac_d),
                         (bsuc, bsuc_d), (ones_s, ones_d), (sel8, sel8_d),
                         (selg, selg_d), (ident, ident_d)):
            dma("Y", dst[:], src, "cst", writes=[b_const], acc=True)
        for dst, src in ((bdn2d, bdn_d), (bsd_r, bsd_d), (bg2d, bg_d),
                         (wg_sb, wg_d)):
            dma("G", dst[:], src, "cst", writes=[b_const], acc=True)

        def ln_tile(t, src_chunk_dma, rstd_ap, negmr_ap, b_rstd, rstd_acc):
            for oc in range(4):
                sl = oc
                src_chunk_dma(oc, sl)
                P.emit("V", lambda eng, t=t, oc=oc, sl=sl: eng.bn_stats(
                    stats[:, t, oc, :], xst[:, sl, :]),
                    reads=[b_xst[sl]], writes=[b_stats[t]], acc=(oc > 0))
            P.emit("V", lambda eng, t=t: eng.bn_aggr(
                mv[:], stats[:, t, :, :].rearrange("p a b -> p (a b)")),
                reads=[b_stats[t]], writes=[b_mv])
            P.emit("V", lambda eng: eng.tensor_scalar(
                lncols[:, 0:1], mv[:, 1:2], EPS, None, ALU.add),
                reads=[b_mv], writes=[b_lncols])
            P.emit("S", lambda eng: eng.activation(
                lncols[:, 3:4], lncols[:, 0:1], AF.Ln),
                reads=[b_lncols], writes=[b_lncols], acc=True)
            P.emit("S", lambda eng, r=rstd_ap: eng.activation(
                r, lncols[:, 3:4], AF.Exp, scale=-0.5),
                reads=[b_lncols], writes=[b_rstd], acc=rstd_acc)
            P.emit("V", lambda eng, r=rstd_ap, n=negmr_ap: eng.tensor_scalar(
                n, mv[:, 0:1], r, -1.0, ALU.mult, ALU.mult),
                reads=[b_mv, b_rstd], writes=[b_rstd], acc=True)

        def emit_ln1(stidx):
            row0 = stidx * ST
            for t in range(NT):
                r0 = row0 + t * 128

                def xload(oc, sl, r0=r0):
                    dma("S", xst[:, sl, :],
                        x_d[r0:r0 + 128, oc * 512:(oc + 1) * 512],
                        f"xld{sl}", writes=[b_xst[sl]])

                ln_tile(t, xload, lncols[:, 1:2], lncols[:, 2:3], b_lncols, True)
                for oc in range(4):
                    P.emit("S", lambda eng, oc=oc: eng.activation(
                        lnbf[:, oc * 512:(oc + 1) * 512], xst[:, oc, :],
                        AF.Identity, bias=lncols[:, 2:3],
                        scale=lncols[:, 1:2]),
                        reads=[b_xst[oc], b_lncols], writes=[b_lnbf],
                        acc=(oc > 0))
                for kc in range(16):
                    dma("Y", fmajT[:, kc, t * 128:(t + 1) * 128],
                        lnbf[:, kc * 128:(kc + 1) * 128],
                        rot("tf", 4), reads=[b_lnbf], writes=[b_fmajT],
                        acc=not (t == 0 and kc == 0), transpose=True)

        def emit_supertile(stidx):
            row0 = stidx * ST

            def attn_scores_sub(t, koc):
                qc = 0 * NT * D + t * D
                kcol = 1 * NT * D + t * D
                for h in range(NH):
                    for k in range(4 * koc, 4 * koc + 4):
                        P.emit("V", lambda eng, h=h, k=k, qc=qc, kcol=kcol,
                               t=t: eng.scalar_tensor_tensor(
                                   scr_v[:],
                                   qkv[:, qc + h * 128:qc + (h + 1) * 128],
                                   1.0,
                                   qkv[:, kcol + k * 128:kcol + (k + 1) * 128],
                                   ALU.mult, ALU.mult,
                                   accum_out=scoresb[:, t, h * 16 + k:
                                                     h * 16 + k + 1]),
                               reads=[b_qkv[0][t], b_qkv[1][t]],
                               writes=[b_scores[t]], noself=True,
                               acc=not (h == 0 and k == 4 * koc and koc == 0))

            def attn_o(t):
                vc = 2 * NT * D + t * D
                for h in range(NH):
                    osl = o32[:, h * 128:(h + 1) * 128]
                    for k in range(NH):
                        col = scoresb[:, t, h * 16 + k:h * 16 + k + 1]
                        vsl = qkv[:, vc + k * 128:vc + (k + 1) * 128]
                        if k == 0:
                            P.emit("V", lambda eng, c=col, v=vsl, o=osl:
                                   eng.tensor_scalar(o, v, c, None, ALU.mult),
                                   reads=[b_scores[t], b_qkv[2][t]],
                                   writes=[b_o32], noself=True,
                                   acc=(h > 0))
                        else:
                            P.emit("V", lambda eng, c=col, v=vsl, o=osl:
                                   eng.scalar_tensor_tensor(
                                       o, v, c, o, ALU.mult, ALU.add),
                                   reads=[b_scores[t], b_qkv[2][t], b_o32],
                                   writes=[b_o32], noself=True, acc=True)

            def attn_proj(wi, wname, wd, sub=None):
                for oc in range(4):
                    slot, sb, fam = nextA()
                    load_w(stidx, slot, sb, fam, wd[oc], scr[wname][oc],
                           b_scr[wname])
                    for t in range(NT):
                        pi = next_pm()
                        for kc in range(16):
                            mm(pm[pi][:], fmajT[:, kc, t * 128:(t + 1) * 128],
                               slot[:, kc, :], kc == 0, kc == 15,
                               [b_fmajT, sb], b_pm[pi])
                        c0 = wi * NT * D + t * D + oc * 512
                        P.emit("S", lambda eng, pi=pi, c0=c0: eng.copy(
                            qkv[:, c0:c0 + 512], pm[pi][:]),
                            reads=[b_pm[pi]], writes=[b_qkv[wi][t]],
                            hazards=al_w[wi], acc=(oc > 0))
                        if sub is not None:
                            sub(t, oc)

            def attn_fin(t):
                P.emit("S", lambda eng: eng.copy(obf[:], o32[:]),
                       reads=[b_o32], writes=[b_obf])
                for kc in range(16):
                    dma("Y", othsu[:, kc, t * 128:(t + 1) * 128],
                        obf[:, kc * 128:(kc + 1) * 128],
                        rot("to", 4), reads=[b_obf], writes=[b_oT[t]],
                        hazards=al_oT, acc=(kc > 0), transpose=True)

            attn_proj(0, "wq", wq_d)
            attn_proj(1, "wk", wk_d, sub=attn_scores_sub)
            attn_proj(2, "wv", wv_d)
            for t in range(NT):
                attn_o(t)
                attn_fin(t)

            def ln2_chain(t):
                r0 = row0 + t * 128
                P.emit("V", lambda eng, t=t: eng.bn_aggr(
                    mv[:], stats[:, t, :, :].rearrange("p a b -> p (a b)")),
                    reads=[b_stats[t]], writes=[b_mv])
                P.emit("V", lambda eng: eng.tensor_scalar(
                    lncols[:, 0:1], mv[:, 1:2], EPS, None, ALU.add),
                    reads=[b_mv], writes=[b_lncols])
                P.emit("S", lambda eng: eng.activation(
                    lncols[:, 3:4], lncols[:, 0:1], AF.Ln),
                    reads=[b_lncols], writes=[b_lncols], acc=True)
                P.emit("S", lambda eng, t=t: eng.activation(
                    m2r2[:, t, 0:1], lncols[:, 3:4], AF.Exp, scale=-0.5),
                    reads=[b_lncols], writes=[b_m2r2], acc=(t > 0))
                P.emit("V", lambda eng, t=t: eng.tensor_scalar(
                    m2r2[:, t, 1:2], mv[:, 0:1], m2r2[:, t, 0:1], -1.0,
                    ALU.mult, ALU.mult),
                    reads=[b_mv, b_m2r2], writes=[b_m2r2], acc=True)
                for oc in range(4):
                    sl = oc
                    dma("S", x2st[:, sl, :],
                        x2scr_d[r0:r0 + 128, oc * 512:(oc + 1) * 512],
                        f"x2ld{sl}", reads=[b_x2scr], writes=[b_x2st[sl]])
                    P.emit("S", lambda eng, t=t, oc=oc, sl=sl: eng.activation(
                        lnbf[:, oc * 512:(oc + 1) * 512], x2st[:, sl, :],
                        AF.Identity, bias=m2r2[:, t, 1:2],
                        scale=m2r2[:, t, 0:1]),
                        reads=[b_x2st[sl], b_m2r2], writes=[b_lnbf],
                        acc=(oc > 0))
                for kc in range(16):
                    dma("Y", fmajT[:, kc, t * 128:(t + 1) * 128],
                        lnbf[:, kc * 128:(kc + 1) * 128],
                        rot("tf", 4), reads=[b_lnbf], writes=[b_fmajT],
                        acc=not (t == 0 and kc == 0), transpose=True)

            for oc in range(4):
                slot, sb, fam = nextA()
                load_w(stidx, slot, sb, fam, wo_d[oc], scr["wo"][oc],
                       b_scr["wo"])
                for t in range(NT):
                    r0 = row0 + t * 128
                    pi = next_pm()
                    for kc in range(16):
                        mm(pm[pi][:], othsu[:, kc, t * 128:(t + 1) * 128],
                           slot[:, kc, :], kc == 0, kc == 15,
                           [b_oT[t], sb], b_pm[pi])
                    sl = t
                    dma("S", xst[:, sl, :],
                        x_d[r0:r0 + 128, oc * 512:(oc + 1) * 512],
                        f"xld{sl}", writes=[b_xst[sl]])
                    P.emit("V", lambda eng, pi=pi, sl=sl: eng.tensor_add(
                        x2st[:, sl, :], xst[:, sl, :], pm[pi][:]),
                        reads=[b_xst[sl], b_pm[pi]], writes=[b_x2st[sl]])
                    P.emit("V", lambda eng, t=t, oc=oc, sl=sl: eng.bn_stats(
                        stats[:, t, oc, :], x2st[:, sl, :]),
                        reads=[b_x2st[sl]], writes=[b_stats[t]], acc=(oc > 0))
                    dma("S", x2scr_d[r0:r0 + 128, oc * 512:(oc + 1) * 512],
                        x2st[:, sl, :], rot("sx", 2), reads=[b_x2st[sl]],
                        writes=[b_x2scr], acc=not (oc == 0 and t == 0))
                    if oc == 3:
                        ln2_chain(t)

            for j in range(2):
                for t in range(NT):
                    for kc in range(16):
                        mm(pm[6][:, 0:8], fmajT[:, kc, t * 128:(t + 1) * 128],
                           wg_sb[:, kc, j * 8:(j + 1) * 8], kc == 0, False,
                           [b_fmajT, b_const], b_pm[6])
                    mm(pm[6][:, 0:8], selg[0:2, j * 128:(j + 1) * 128],
                       bg2d[0:2, :], False, True, [b_const], b_pm[6])
                    P.emit("S", lambda eng, t=t: eng.copy(
                        g8[:, t, :], pm[6][:, 0:8]),
                        reads=[b_pm[6]], writes=[b_g8[t]])
                    gap = g8[:, t, :]
                    P.emit("V", lambda eng, g=gap: eng.tensor_reduce(
                        rcol[:, 0:1], g, AX.X, ALU.min),
                        reads=[b_g8[t]], writes=[b_rcol])
                    P.emit("V", lambda eng, g=gap: eng.tensor_scalar(
                        rtmp[:, 0, :], g, rcol[:, 0:1], None, ALU.is_equal),
                        reads=[b_g8[t], b_rcol], writes=[b_rtmp])
                    P.emit("V", lambda eng, g=gap: eng.scalar_tensor_tensor(
                        rtmp[:, 1, :], rtmp[:, 0, :], 1e30, g,
                        ALU.mult, ALU.add),
                        reads=[b_g8[t], b_rtmp], writes=[b_rtmp], acc=True)
                    P.emit("V", lambda eng: eng.tensor_reduce(
                        rcol[:, 1:2], rtmp[:, 1, :], AX.X, ALU.min),
                        reads=[b_rtmp], writes=[b_rcol], acc=True)
                    P.emit("V", lambda eng, g=gap: eng.tensor_scalar(
                        rtmp[:, 1, :], g, rcol[:, 1:2], None, ALU.is_gt),
                        reads=[b_g8[t], b_rcol], writes=[b_rtmp], acc=True)
                    P.emit("V", lambda eng, g=gap: eng.reduce_max(
                        rcol[:, 2:3], g, AX.X, negate=True),
                        reads=[b_g8[t]], writes=[b_rcol], acc=True)
                    P.emit("S", lambda eng, g=gap: eng.activation(
                        rtmp[:, 2, :], g, AF.Exp, bias=rcol[:, 2:3]),
                        reads=[b_g8[t], b_rcol], writes=[b_rtmp], acc=True)
                    P.emit("V", lambda eng: eng.tensor_mul(
                        rtmp[:, 0, :], rtmp[:, 2, :], rtmp[:, 1, :]),
                        reads=[b_rtmp], writes=[b_rtmp], acc=True)
                    P.emit("V", lambda eng: eng.tensor_reduce(
                        rcol[:, 3:4], rtmp[:, 0, :], AX.X, ALU.add),
                        reads=[b_rtmp], writes=[b_rcol], acc=True)
                    P.emit("V", lambda eng: eng.reciprocal(
                        rcol[:, 4:5], rcol[:, 3:4]),
                        reads=[b_rcol], writes=[b_rcol], acc=True)
                    P.emit("V", lambda eng, t=t: eng.tensor_scalar(
                        w_r[:, t, :], rtmp[:, 0, :], rcol[:, 4:5], None,
                        ALU.mult),
                        reads=[b_rtmp, b_rcol], writes=[b_w[t]])
                    P.emit("S", lambda eng, t=t: eng.copy(wbf[:], w_r[:, t, :]),
                           reads=[b_w[t]], writes=[b_wbf])
                    P.emit("T", lambda eng: eng.transpose(
                        ptp[0:8, 0:128], wbf[:], ident[:]),
                        reads=[b_wbf, b_const], writes=[b_ptp])
                    P.emit("S", lambda eng, t=t: eng.copy(
                        wT[0:8, t * 128:(t + 1) * 128], ptp[0:8, 0:128]),
                        reads=[b_ptp], writes=[b_wT], acc=(t > 0))

                for ex in range(6):
                    bws = ex % 2
                    mm(pm[6][:, 0:512], sel8[0:8, ex * 128:(ex + 1) * 128],
                       wT[0:8, :], True, True, [b_const, b_wT], b_pm[6])
                    P.emit("S", lambda eng, s=bws: eng.copy(
                        bw_sb[s][:], pm[6][:, 0:512]),
                        reads=[b_pm[6]], writes=[b_bw[bws]])
                    for och in range(2):
                        slot, sb, fam = nextA()
                        load_w(stidx, slot, sb, fam, wup_d[ex, och],
                               scr["wup"][ex, och], b_scr["wup"])
                        for ofc in range(4):
                            pi = next_pm()
                            for kc in range(16):
                                mm(pm[pi][:],
                                   slot[:, kc, ofc * 128:(ofc + 1) * 128],
                                   fmajT[:, kc, :], kc == 0, kc == 15,
                                   [b_fmajT, sb], b_pm[pi])
                            cc = och * 4 + ofc
                            P.emit("V", lambda eng, pi=pi, cc=cc, ex=ex:
                                   eng.tensor_scalar(
                                       othsu[:, cc, :], pm[pi][:],
                                       bupc[:, ex * 8 + cc:ex * 8 + cc + 1],
                                       None, ALU.add),
                                   reads=[b_pm[pi], b_const],
                                   writes=[b_hF], hazards=al_h,
                                   acc=(och * 4 + ofc > 0))
                    for och in range(2):
                        slot, sb, fam = nextB()
                        load_w(stidx, slot, sb, fam, wsw_d[ex, och],
                               scr["wsw"][ex, och], b_scr["wsw"])
                        for ofc in range(4):
                            pi = next_pm()
                            for kc in range(8):
                                mm(pm[pi][:],
                                   slot[:, kc, ofc * 128:(ofc + 1) * 128],
                                   othsu[:, kc, :], kc == 0, kc == 7,
                                   [b_hF, sb], b_pm[pi])
                            cc = och * 4 + ofc
                            P.emit("S", lambda eng, pi=pi, cc=cc, ex=ex:
                                   eng.activation(
                                       othsu[:, 8 + cc, :], pm[pi][:],
                                       AF.Silu,
                                       bias=bswc[:, ex * 8 + cc:
                                                 ex * 8 + cc + 1]),
                                   reads=[b_pm[pi], b_const],
                                   writes=[b_suF], hazards=al_su,
                                   acc=(och * 4 + ofc > 0))
                    gc0 = ex * 4096
                    for cc in range(8):
                        P.emit("V", lambda eng, cc=cc: eng.tensor_mul(
                            othsu[:, 8 + cc, :], othsu[:, 8 + cc, :],
                            othsu[:, cc, :]),
                            reads=[b_hF, b_suF], writes=[b_suF], acc=True,
                            noself=True)
                        P.emit("V", lambda eng, cc=cc, gc0=gc0, s=bws:
                               eng.tensor_mul(
                                   qkv[:, gc0 + cc * 512:gc0 + (cc + 1) * 512],
                                   othsu[:, 8 + cc, :], bw_sb[s][:]),
                               reads=[b_suF, b_bw[bws]], writes=[b_gf[ex]],
                               hazards=al_gf[ex], acc=(cc > 0), noself=True)

                for c in range(2):
                    for q2 in range(2):
                        slot, sb, fam = nextA()
                        ocu = c * 2 + q2
                        load_w(stidx, slot, sb, fam, wsu_d[ocu],
                               scr["wsu"][ocu], b_scr["wsu"])
                        for ofc in range(4):
                            pi = next_pm()
                            for kc in range(16):
                                mm(pm[pi][:],
                                   slot[:, kc, ofc * 128:(ofc + 1) * 128],
                                   fmajT[:, kc, :], kc == 0, kc == 15,
                                   [b_fmajT, sb], b_pm[pi])
                            cc = q2 * 4 + ofc
                            P.emit("V", lambda eng, pi=pi, cc=cc, c=c:
                                   eng.tensor_scalar(
                                       shuT[:, cc, :], pm[pi][:],
                                       bsuc[:, c * 8 + cc:c * 8 + cc + 1],
                                       None, ALU.add),
                                   reads=[b_pm[pi], b_const], writes=[b_shuT],
                                   acc=(q2 * 4 + ofc > 0))
                    for och in range(2):
                        slot, sb, fam = nextB()
                        load_w(stidx, slot, sb, fam, wsa_d[och],
                               scr["wsa"][och], b_scr["wsa"])
                        for ofc in range(4):
                            pi = next_pm()
                            for kc in range(8):
                                mm(pm[pi][:],
                                   slot[:, kc, ofc * 128:(ofc + 1) * 128],
                                   shuT[:, kc, :], kc == 0, kc == 7,
                                   [b_shuT, sb], b_pm[pi])
                            cc = och * 4 + ofc
                            P.emit("S", lambda eng, pi=pi, cc=cc:
                                   eng.activation(
                                       othsu[:, cc, :], pm[pi][:], AF.Silu,
                                       bias=bsac[:, cc:cc + 1]),
                                   reads=[b_pm[pi], b_const], writes=[b_hF],
                                   hazards=al_h, acc=(och * 4 + ofc > 0))
                    for cc in range(8):
                        P.emit("V", lambda eng, cc=cc, c=c: eng.tensor_mul(
                            swF[:, c * 8 + cc, :], othsu[:, cc, :],
                            shuT[:, cc, :]),
                            reads=[b_hF, b_shuT], writes=[b_swF],
                            acc=not (c == 0 and cc == 0))

                if j == 1 and stidx + 1 < nst:
                    emit_ln1(stidx + 1)
                for oc in range(4):
                    pis = [0, 1, 2, 3] if oc % 2 == 0 else [4, 5, 6, 0]
                    for t in range(NT):
                        r0 = row0 + t * 128
                        dma("S", x2st[:, t, :],
                            x2scr_d[r0:r0 + 128, oc * 512:(oc + 1) * 512],
                            f"x2ld{t}", reads=[b_x2scr], writes=[b_x2st[t]])
                    for ei in range(6):
                        slot, sb, fam = nextB()
                        load_w(stidx, slot, sb, fam, wdn_d[ei, oc],
                               scr["wdn"][ei, oc], b_scr["wdn"])
                        gv = gf_view(ei)
                        for t in range(NT):
                            for kc in range(8):
                                mm(pm[pis[t]][:],
                                   gv[:, kc, t * 128:(t + 1) * 128],
                                   slot[:, kc, :], ei == 0 and kc == 0, False,
                                   [b_gf[ei], sb], b_pm[pis[t]])
                    slot, sb, fam = nextA()
                    load_w(stidx, slot, sb, fam, wsd_d[oc], scr["wsd"][oc],
                           b_scr["wsd"])
                    for t in range(NT):
                        for kc in range(16):
                            mm(pm[pis[t]][:],
                               swF[:, kc, t * 128:(t + 1) * 128],
                               slot[:, kc, :], False, False,
                               [b_swF, sb], b_pm[pis[t]])
                    for t in range(NT):
                        mm(pm[pis[t]][:], wT[0:6, t * 128:(t + 1) * 128],
                           bdn2d[0:6, oc * 512:(oc + 1) * 512], False, False,
                           [b_wT, b_const], b_pm[pis[t]])
                        mm(pm[pis[t]][:], ones_s[0:1, :],
                           bsd_r[0:1, oc * 512:(oc + 1) * 512], False, True,
                           [b_const], b_pm[pis[t]])
                    for t in range(NT):
                        r0 = row0 + t * 128
                        pi = pis[t]
                        if j == 0:
                            sl = t % 2
                            P.emit("S", lambda eng, t=t, sl=sl: eng.activation(
                                xt1bf[:, sl, :], x2st[:, t, :], AF.Identity,
                                bias=m2r2[:, t, 1:2], scale=m2r2[:, t, 0:1]),
                                reads=[b_x2st[t], b_m2r2],
                                writes=[b_xt1bf[sl]])
                            P.emit("V", lambda eng, pi=pi, sl=sl:
                                   eng.tensor_add(
                                       xt1bf[:, sl, :], xt1bf[:, sl, :],
                                       pm[pi][:]),
                                   reads=[b_pm[pi], b_xt1bf[sl]],
                                   writes=[b_xt1bf[sl]], acc=True)
                            for kq in range(4):
                                dma("Y", fmajT[:, oc * 4 + kq,
                                               t * 128:(t + 1) * 128],
                                    xt1bf[:, sl, kq * 128:(kq + 1) * 128],
                                    rot("tf", 4),
                                    reads=[b_xt1bf[sl]], writes=[b_fmajT],
                                    acc=not (oc == 0 and t == 0 and kq == 0),
                                    transpose=True)
                        else:
                            sl = t % 2
                            P.emit("S", lambda eng, t=t, sl=sl: eng.activation(
                                outst[:, sl, :], x2st[:, t, :], AF.Identity,
                                bias=m2r2[:, t, 1:2], scale=m2r2[:, t, 0:1]),
                                reads=[b_x2st[t], b_m2r2],
                                writes=[b_outst[sl]])
                            P.emit("V", lambda eng, pi=pi, sl=sl:
                                   eng.tensor_add(
                                       outst[:, sl, :], outst[:, sl, :],
                                       pm[pi][:]),
                                   reads=[b_pm[pi], b_outst[sl]],
                                   writes=[b_outst[sl]], acc=True)
                            P.emit("V", lambda eng, t=t, sl=sl: eng.tensor_add(
                                outst[:, sl, :], outst[:, sl, :],
                                x2st[:, t, :]),
                                reads=[b_x2st[t], b_outst[sl]],
                                writes=[b_outst[sl]], acc=True)
                            dma("S", y_d[r0:r0 + 128, oc * 512:(oc + 1) * 512],
                                outst[:, sl, :], rot("sy", 2),
                                reads=[b_outst[sl]])

        emit_ln1(0)
        for stidx in range(nst):
            emit_supertile(stidx)

        P.barrier("S", "sy0", P.cnt["sy0"])
        P.barrier("S", "sy1", P.cnt["sy1"])
        P.barrier("Y", "sy0", P.cnt["sy0"])
        P.barrier("Y", "sy1", P.cnt["sy1"])

        with nc.Block() as block:
            @block.tensor
            def _(eng):
                P.replay("T", eng)

            @block.vector
            def _(eng):
                P.replay("V", eng)

            @block.scalar
            def _(eng):
                P.replay("S", eng)

            @block.sync
            def _(eng):
                P.replay("Y", eng)

            @block.gpsimd
            def _(eng):
                P.replay("G", eng)

    return nc



def _chunkK(w):
    K, F = w.shape
    return np.ascontiguousarray(
        w.reshape(K // 128, 128, F // 512, 512).transpose(2, 1, 0, 3))


def _prep_inputs(inp):
    f32 = lambda a: np.asarray(a, dtype=np.float32)
    wq = f32(inp["L0_attn_Wq"]); wk = f32(inp["L0_attn_Wk"])
    wv = f32(inp["L0_attn_Wv"]); wo = f32(inp["L0_attn_Wo"])
    wg = f32(inp["L0_ffn_Wg"]); bg = f32(inp["L0_ffn_bg"])
    wup = f32(inp["L0_ffn_Wup"]); bup = f32(inp["L0_ffn_bup"])
    wsw = f32(inp["L0_ffn_Wsw"]); bsw = f32(inp["L0_ffn_bsw"])
    wdn = f32(inp["L0_ffn_Wdn"]); bdn = f32(inp["L0_ffn_bdn"])
    wsu = f32(inp["L0_ffn_Wsu"]); bsu = f32(inp["L0_ffn_bsu"])
    wsa = f32(inp["L0_ffn_Wsa"]); bsa = f32(inp["L0_ffn_bsa"])
    wsd = f32(inp["L0_ffn_Wsd"]); bsd = f32(inp["L0_ffn_bsd"])

    bf = ml_dtypes.bfloat16
    sel8 = np.zeros((8, 1024), dtype=bf)
    for ei in range(8):
        sel8[ei, ei * 128:(ei + 1) * 128] = 1
    selg = np.zeros((2, 256), dtype=bf)
    for ji in range(2):
        selg[ji, ji * 128:(ji + 1) * 128] = 1

    shared = {
        "wq_c": _chunkK(wq), "wk_c": _chunkK(wk), "wv_c": _chunkK(wv),
        "wo_c": _chunkK(wo),
        "wup_c": np.stack([_chunkK(wup[i]) for i in range(6)]),
        "wsw_c": np.stack([_chunkK(wsw[i]) for i in range(6)]),
        "wdn_c": np.stack([_chunkK(wdn[i]) for i in range(6)]),
        "wsu_c": _chunkK(wsu), "wsa_c": _chunkK(wsa), "wsd_c": _chunkK(wsd),
        "wg_c": np.ascontiguousarray(
            wg.reshape(2, 16, 128, 8).transpose(2, 1, 0, 3).reshape(128, 16, 16)),
        "bupc": np.ascontiguousarray(
            bup.reshape(6, 8, 128).transpose(2, 0, 1).reshape(128, 48)),
        "bswc": np.ascontiguousarray(
            bsw.reshape(6, 8, 128).transpose(2, 0, 1).reshape(128, 48)),
        "bsac": np.ascontiguousarray(bsa.reshape(8, 128).T),
        "bsuc": np.ascontiguousarray(bsu.reshape(16, 128).T),
        "bdn2d": np.ascontiguousarray(bdn),
        "bsd_r": np.ascontiguousarray(bsd[None, :]),
        "bg2d": np.ascontiguousarray(bg),
        "ones_c": np.ones((1, 128), dtype=bf),
        "sel8_c": sel8, "selg_c": selg,
        "ident_c": np.eye(128, dtype=bf),
    }
    return shared


def kernel(**inputs):
    x = np.asarray(inputs["L0_attn_x"], dtype=np.float32)
    B, N, _ = x.shape
    xf = np.ascontiguousarray(x.reshape(B * N, D))
    shared = _prep_inputs(inputs)
    nc = build_program(TPC)
    in_maps = [dict(shared, x=np.ascontiguousarray(xf[c * TPC:(c + 1) * TPC]))
               for c in range(N_CORES)]
    res = run_bass_kernel_spmd(nc, in_maps, core_ids=list(range(N_CORES)))
    y = np.concatenate([r["y"] for r in res.results], axis=0)
    return y.reshape(B, N, D)


# version 09
# speedup vs baseline: 1.2282x; 1.0176x over previous
import numpy as np
import ml_dtypes
import concourse.bass as bass
import concourse.mybir as mybir
from concourse.bass_utils import run_bass_kernel_spmd

F32 = mybir.dt.float32
BF16 = mybir.dt.bfloat16
AF = mybir.ActivationFunctionType
ALU = mybir.AluOpType
AX = mybir.AxisListType

N_CORES = 8
TPC = 4096
ST = 512
NT = 4
D = 2048
NH = 16
EPS = 1e-5



class Buf:
    __slots__ = ("name", "writers", "readers")

    def __init__(self, name):
        self.name = name
        self.writers = {}
        self.readers = {}


class Ins:
    __slots__ = ("waits", "build", "incs")

    def __init__(self, waits, build, incs):
        self.waits, self.build, self.incs = waits, build, incs


class Prog:
    STREAMS = ("T", "V", "S", "Y", "G")

    def __init__(self, nc, sems):
        self.nc = nc
        self.sems = sems
        self.cnt = {k: 0 for k in sems}
        self.ir = {s: [] for s in self.STREAMS}
        self.waited = {s: {} for s in self.STREAMS}

    def emit(self, stream, build, reads=(), writes=(), hazards=(), dma=False,
             acc=False, dsem=None, noself=False):
        if dma:
            sem = dsem
            inc = 16
        else:
            sem = stream
            inc = 1
        waits = {}
        if dma and self.cnt[sem] > 0 and \
                self.waited[stream].get(sem, -1) < self.cnt[sem]:
            waits[sem] = self.cnt[sem]

        def need(dep):
            for s, v in dep.items():
                if stream == "T" and s == "T" and not dma:
                    continue
                if noself and s == stream and not dma:
                    continue
                if self.waited[stream].get(s, -1) >= v:
                    continue
                if waits.get(s, -1) < v:
                    waits[s] = v

        for b in reads:
            need(b.writers)
        if not acc:
            for b in writes:
                need(b.writers)
                need(b.readers)
            for b in hazards:
                need(b.writers)
                need(b.readers)
        nv = self.cnt[sem] + inc
        self.cnt[sem] = nv
        for s, v in waits.items():
            assert v <= self.cnt[s], f"wait on future {s}:{v}>{self.cnt[s]}"
            if self.waited[stream].get(s, -1) < v:
                self.waited[stream][s] = v
        for b in reads:
            if b.readers.get(sem, -1) < nv:
                b.readers[sem] = nv
        for b in writes:
            if acc:
                b.writers[sem] = max(b.writers.get(sem, -1), nv)
            else:
                b.writers = {sem: nv}
                b.readers = {}
        self.ir[stream].append(
            Ins([(self.sems[s], v) for s, v in sorted(waits.items())],
                build, [(self.sems[sem], inc)]))

    def barrier(self, stream, sem_name, value):
        self.ir[stream].append(Ins([(self.sems[sem_name], value)], None, []))

    def replay(self, stream, eng):
        for ins in self.ir[stream]:
            for sem, v in ins.waits:
                eng.wait_ge(sem, v)
            if ins.build is not None:
                bi = ins.build(eng)
                for sem, d in ins.incs:
                    bi.then_inc(sem, d)



def build_program(n_tok, dbg_x2=False):
    nst = n_tok // ST
    nc = bass.Bass("TRN2", target_bir_lowering=False)

    x_d = nc.dram_tensor("x", [n_tok, D], F32, kind="ExternalInput").ap()
    y_d = nc.dram_tensor("y", [n_tok, D], F32, kind="ExternalOutput").ap()

    def din(name, shape, dt=F32):
        return nc.dram_tensor(name, shape, dt, kind="ExternalInput").ap()

    wq_d = din("wq_c", [4, 128, 16, 512])
    wk_d = din("wk_c", [4, 128, 16, 512])
    wv_d = din("wv_c", [4, 128, 16, 512])
    wo_d = din("wo_c", [4, 128, 16, 512])
    wup_d = din("wup_c", [6, 2, 128, 16, 512])
    wsw_d = din("wsw_c", [6, 2, 128, 8, 512])
    wdn_d = din("wdn_c", [6, 4, 128, 8, 512])
    wsu_d = din("wsu_c", [4, 128, 16, 512])
    wsa_d = din("wsa_c", [2, 128, 8, 512])
    wsd_d = din("wsd_c", [4, 128, 16, 512])
    wg_d = din("wg_c", [128, 16, 16])
    bupc_d = din("bupc", [128, 48])
    bswc_d = din("bswc", [128, 48])
    bsac_d = din("bsac", [128, 8])
    bsuc_d = din("bsuc", [128, 16])
    bdn_d = din("bdn2d", [6, D])
    bsd_d = din("bsd_r", [1, D])
    bg_d = din("bg2d", [2, 8])
    ones_d = din("ones_c", [1, 128], BF16)
    sel8_d = din("sel8_c", [8, 1024], BF16)
    selg_d = din("selg_c", [2, 256], BF16)
    ident_d = din("ident_c", [128, 128], BF16)

    x2scr_d = nc.dram_tensor("x2scr", [n_tok, D], F32,
                             kind="ExternalOutput" if dbg_x2 else "Internal").ap()
    scr = {}
    for nm, shp in (("wq", [4, 128, 16, 512]), ("wk", [4, 128, 16, 512]),
                    ("wv", [4, 128, 16, 512]), ("wo", [4, 128, 16, 512]),
                    ("wup", [6, 2, 128, 16, 512]), ("wsw", [6, 2, 128, 8, 512]),
                    ("wdn", [6, 4, 128, 8, 512]), ("wsu", [4, 128, 16, 512]),
                    ("wsa", [2, 128, 8, 512]), ("wsd", [4, 128, 16, 512])):
        scr[nm] = nc.dram_tensor("scr_" + nm, shp, BF16, kind="Internal").ap()

    from contextlib import ExitStack
    with ExitStack() as ctx:
        e = ctx.enter_context

        fmajT = e(nc.sbuf_tensor("fmajT", [128, 16, ST], BF16))
        qkv = e(nc.sbuf_tensor("qkv", [128, 3 * NT * D], BF16))
        othsu = e(nc.sbuf_tensor("othsu", [128, 16, ST], BF16))
        o32 = e(nc.sbuf_tensor("o32", [128, D], F32))
        lnbf = e(nc.sbuf_tensor("lnbf", [128, D], BF16))
        obf = lnbf
        scoresb = e(nc.sbuf_tensor("scoresb", [128, NT, 256], F32))
        scr_v = e(nc.sbuf_tensor("scr_v", [128, 128], BF16))
        shuT = e(nc.sbuf_tensor("shuT", [128, 8, ST], BF16))
        swF = e(nc.sbuf_tensor("swF", [128, 16, ST], BF16))
        xst = e(nc.sbuf_tensor("xst", [128, 4, 512], F32))
        x2st = e(nc.sbuf_tensor("x2st", [128, 4, 512], F32))
        outst = e(nc.sbuf_tensor("outst", [128, 2, 512], F32))
        xt1bf = e(nc.sbuf_tensor("xt1bf", [128, 2, 512], BF16))
        wA = [e(nc.sbuf_tensor(f"wA{i}", [128, 16, 512], BF16)) for i in range(2)]
        wB = [e(nc.sbuf_tensor(f"wB{i}", [128, 8, 512], BF16)) for i in range(2)]
        bw_sb = [e(nc.sbuf_tensor(f"bw{i}", [128, 512], BF16)) for i in range(2)]
        wg_sb = e(nc.sbuf_tensor("wg_sb", [128, 16, 16], BF16))
        bupc = e(nc.sbuf_tensor("bupc_s", [128, 48], F32))
        bswc = e(nc.sbuf_tensor("bswc_s", [128, 48], F32))
        bsac = e(nc.sbuf_tensor("bsac_s", [128, 8], F32))
        bsuc = e(nc.sbuf_tensor("bsuc_s", [128, 16], F32))
        bdn2d = e(nc.sbuf_tensor("bdn2d_s", [6, D], BF16))
        bsd_r = e(nc.sbuf_tensor("bsd_s", [1, D], BF16))
        bg2d = e(nc.sbuf_tensor("bg2d_s", [2, 8], BF16))
        ones_s = e(nc.sbuf_tensor("ones_s", [1, 128], BF16))
        sel8 = e(nc.sbuf_tensor("sel8_s", [8, 1024], BF16))
        selg = e(nc.sbuf_tensor("selg_s", [2, 256], BF16))
        ident = e(nc.sbuf_tensor("ident_s", [128, 128], BF16))
        stats = e(nc.sbuf_tensor("stats", [128, NT, 4, 6], F32))
        mv = e(nc.sbuf_tensor("mv", [128, 2], F32))
        lncols = e(nc.sbuf_tensor("lncols", [128, 4], F32))
        m2r2 = e(nc.sbuf_tensor("m2r2", [128, NT, 2], F32))
        g8 = e(nc.sbuf_tensor("g8", [128, NT, 8], F32))
        rtmp = e(nc.sbuf_tensor("rtmp", [128, 3, 8], F32))
        rcol = e(nc.sbuf_tensor("rcol", [128, 8], F32))
        w_r = e(nc.sbuf_tensor("w_r", [128, NT, 8], F32))
        wbf = e(nc.sbuf_tensor("wbf", [128, 8], BF16))
        wT = e(nc.sbuf_tensor("wT", [8, ST], BF16))

        pm = [e(nc.psum_tensor(f"pm{i}", [128, 512], F32)) for i in range(7)]
        ptp = e(nc.psum_tensor("ptp", [128, 1024], BF16))

        dma_fams = (["cst", "xld0", "xld1", "xld2", "xld3",
                     "x2ld0", "x2ld1", "x2ld2", "x2ld3",
                     "wA0", "wA1", "wB0", "wB1",
                     "scst0", "scst1", "scst2",
                     "sx0", "sx1", "sy0", "sy1"]
                    + [f"tf{i}" for i in range(4)]
                    + [f"to{i}" for i in range(4)])
        sems = {k: e(nc.semaphore(f"sem_{k}"))
                for k in ["T", "V", "S", "G"] + dma_fams}
        P = Prog(nc, sems)

        b_fmajT = [Buf(f"fmajT{t}") for t in range(NT)]
        b_qkv = [[Buf(f"qkv{w}_{t}") for t in range(NT)] for w in range(3)]
        b_gf = [Buf(f"gF{i}") for i in range(6)]
        b_oT = [Buf(f"oT{t}") for t in range(NT)]
        b_hF, b_suF = Buf("hF"), Buf("suF")
        b_o32, b_lnbf = Buf("o32"), Buf("lnbf")
        b_obf = b_lnbf
        b_scores = [Buf(f"sc{t}") for t in range(NT)]
        b_shuT, b_swF = Buf("shuT"), Buf("swF")
        b_xst = [Buf(f"xst{i}") for i in range(4)]
        b_x2st = [Buf(f"x2st{i}") for i in range(4)]
        b_outst = [Buf("out0"), Buf("out1")]
        b_xt1bf = [Buf("xt1b0"), Buf("xt1b1")]
        b_wA = [Buf("wA0"), Buf("wA1")]
        b_wB = [Buf("wB0"), Buf("wB1")]
        b_bw = [Buf("bw0"), Buf("bw1")]
        b_pm = [Buf(f"pm{i}") for i in range(7)]
        b_ptp = Buf("ptp")
        b_stats = [Buf(f"st{t}") for t in range(NT)]
        b_mv, b_lncols, b_m2r2 = Buf("mv"), Buf("lncols"), Buf("m2r2")
        b_g8 = [Buf(f"g8_{t}") for t in range(NT)]
        b_rtmp, b_rcol = Buf("rtmp"), Buf("rcol")
        b_w = [Buf(f"w{t}") for t in range(NT)]
        b_wbf, b_wT = Buf("wbf"), Buf("wT")
        b_x2scr = Buf("x2scr")
        b_scr = {k: Buf("scr_" + k) for k in scr}
        b_const = Buf("const")

        al_w = {0: [b_gf[0], b_gf[1]], 1: [b_gf[2], b_gf[3]],
                2: [b_gf[4], b_gf[5]]}
        al_gf = {e_: [b_qkv[e_ // 2][2 * (e_ % 2)],
                      b_qkv[e_ // 2][2 * (e_ % 2) + 1]] for e_ in range(6)}
        al_oT = [b_hF, b_suF]
        al_h = list(b_oT)
        al_su = list(b_oT)

        def gf_view(ei):
            return qkv[:, ei * 4096:(ei + 1) * 4096].rearrange(
                "p (k c) -> p k c", c=512)

        rots = {"tf": 0, "to": 0, "scst": 0, "sx": 0, "sy": 0}

        def rot(fam, n):
            i = rots[fam] % n
            rots[fam] += 1
            return f"{fam}{i}"

        def pe_tp4(dst3d, src2d, kc0, reads, writes, acc):
            for i in range(4):
                src = src2d[:, (kc0 + i) * 128:(kc0 + i + 1) * 128]
                P.emit("T", lambda eng, i=i, src=src: eng.transpose(
                    ptp[:, (i + 1) * 128:(i + 2) * 128], src, ident[:]),
                    reads=list(reads) + [b_const], writes=[b_ptp],
                    acc=(i > 0))
            P.emit("S", lambda eng: eng.copy(
                dst3d, ptp[:, 128:640].rearrange("p (k c) -> p k c", c=128)),
                reads=[b_ptp], writes=writes, acc=acc)

        def dma(stream, out_ap, in_ap, dsem, reads=(), writes=(), hazards=(),
                acc=False, transpose=False):
            if transpose:
                bld = (lambda eng, o=out_ap, i=in_ap:
                       eng.dma_start_transpose(o, i))
            else:
                bld = (lambda eng, o=out_ap, i=in_ap:
                       eng.dma_start(out=o, in_=i))
            P.emit(stream, bld, reads=reads, writes=writes, hazards=hazards,
                   dma=True, acc=acc, dsem=dsem)

        def mm(psum_ap, lhsT_ap, rhs_ap, start, stop, reads, pbuf):
            bld = (lambda eng, o=psum_ap, l=lhsT_ap, r=rhs_ap, a=start, b=stop:
                   eng.matmul(o, l, r, start=a, stop=b))
            P.emit("T", bld, reads=reads, writes=[pbuf], acc=not start)

        def load_w(stidx, slot_sb, slot_buf, fam, f32_ap, scr_ap, scr_buf):
            if stidx == 0:
                dma("G", slot_sb[:], f32_ap, fam, reads=[b_const],
                    writes=[slot_buf])
                dma("G", scr_ap, slot_sb[:], rot("scst", 3), reads=[slot_buf],
                    writes=[scr_buf], acc=True)
            else:
                dma("G", slot_sb[:], scr_ap, fam, reads=[scr_buf],
                    writes=[slot_buf])

        ring = {"A": 0, "B": 0}

        def nextA():
            i = ring["A"] % 2
            ring["A"] += 1
            return wA[i], b_wA[i], f"wA{i}"

        def nextB():
            i = ring["B"] % 2
            ring["B"] += 1
            return wB[i], b_wB[i], f"wB{i}"

        pmrot = [0]

        def next_pm():
            i = pmrot[0] % 6
            pmrot[0] += 1
            return i

        for dst, src in ((bupc, bupc_d), (bswc, bswc_d), (bsac, bsac_d),
                         (bsuc, bsuc_d), (ones_s, ones_d), (sel8, sel8_d),
                         (selg, selg_d), (ident, ident_d)):
            dma("Y", dst[:], src, "cst", writes=[b_const], acc=True)
        for dst, src in ((bdn2d, bdn_d), (bsd_r, bsd_d), (bg2d, bg_d),
                         (wg_sb, wg_d)):
            dma("G", dst[:], src, "cst", writes=[b_const], acc=True)

        def ln_tile(t, src_chunk_dma, rstd_ap, negmr_ap, b_rstd, rstd_acc):
            for oc in range(4):
                sl = oc
                src_chunk_dma(oc, sl)
                P.emit("V", lambda eng, t=t, oc=oc, sl=sl: eng.bn_stats(
                    stats[:, t, oc, :], xst[:, sl, :]),
                    reads=[b_xst[sl]], writes=[b_stats[t]], acc=(oc > 0))
            P.emit("V", lambda eng, t=t: eng.bn_aggr(
                mv[:], stats[:, t, :, :].rearrange("p a b -> p (a b)")),
                reads=[b_stats[t]], writes=[b_mv])
            P.emit("V", lambda eng: eng.tensor_scalar(
                lncols[:, 0:1], mv[:, 1:2], EPS, None, ALU.add),
                reads=[b_mv], writes=[b_lncols])
            P.emit("S", lambda eng: eng.activation(
                lncols[:, 3:4], lncols[:, 0:1], AF.Ln),
                reads=[b_lncols], writes=[b_lncols], acc=True)
            P.emit("S", lambda eng, r=rstd_ap: eng.activation(
                r, lncols[:, 3:4], AF.Exp, scale=-0.5),
                reads=[b_lncols], writes=[b_rstd], acc=rstd_acc)
            P.emit("V", lambda eng, r=rstd_ap, n=negmr_ap: eng.tensor_scalar(
                n, mv[:, 0:1], r, -1.0, ALU.mult, ALU.mult),
                reads=[b_mv, b_rstd], writes=[b_rstd], acc=True)

        def emit_ln1(stidx):
            row0 = stidx * ST
            for t in range(NT):
                r0 = row0 + t * 128

                def xload(oc, sl, r0=r0):
                    dma("S", xst[:, sl, :],
                        x_d[r0:r0 + 128, oc * 512:(oc + 1) * 512],
                        f"xld{sl}", writes=[b_xst[sl]])

                ln_tile(t, xload, lncols[:, 1:2], lncols[:, 2:3], b_lncols, True)
                for oc in range(4):
                    P.emit("S", lambda eng, oc=oc: eng.activation(
                        lnbf[:, oc * 512:(oc + 1) * 512], xst[:, oc, :],
                        AF.Identity, bias=lncols[:, 2:3],
                        scale=lncols[:, 1:2]),
                        reads=[b_xst[oc], b_lncols], writes=[b_lnbf],
                        acc=(oc > 0))
                for kc in range(16):
                    dma("Y", fmajT[:, kc, t * 128:(t + 1) * 128],
                        lnbf[:, kc * 128:(kc + 1) * 128],
                        rot("tf", 4), reads=[b_lnbf], writes=[b_fmajT[t]],
                        acc=(kc > 0), transpose=True)

        def emit_supertile(stidx):
            row0 = stidx * ST

            def attn_scores_sub(t, koc):
                qc = 0 * NT * D + t * D
                kcol = 1 * NT * D + t * D
                for h in range(NH):
                    for k in range(4 * koc, 4 * koc + 4):
                        P.emit("V", lambda eng, h=h, k=k, qc=qc, kcol=kcol,
                               t=t: eng.scalar_tensor_tensor(
                                   scr_v[:],
                                   qkv[:, qc + h * 128:qc + (h + 1) * 128],
                                   1.0,
                                   qkv[:, kcol + k * 128:kcol + (k + 1) * 128],
                                   ALU.mult, ALU.mult,
                                   accum_out=scoresb[:, t, h * 16 + k:
                                                     h * 16 + k + 1]),
                               reads=[b_qkv[0][t], b_qkv[1][t]],
                               writes=[b_scores[t]], noself=True,
                               acc=not (h == 0 and k == 4 * koc and koc == 0))

            def attn_o(t):
                vc = 2 * NT * D + t * D
                for h in range(NH):
                    osl = o32[:, h * 128:(h + 1) * 128]
                    for k in range(NH):
                        col = scoresb[:, t, h * 16 + k:h * 16 + k + 1]
                        vsl = qkv[:, vc + k * 128:vc + (k + 1) * 128]
                        if k == 0:
                            P.emit("V", lambda eng, c=col, v=vsl, o=osl:
                                   eng.tensor_scalar(o, v, c, None, ALU.mult),
                                   reads=[b_scores[t], b_qkv[2][t]],
                                   writes=[b_o32], noself=True,
                                   acc=(h > 0))
                        else:
                            P.emit("V", lambda eng, c=col, v=vsl, o=osl:
                                   eng.scalar_tensor_tensor(
                                       o, v, c, o, ALU.mult, ALU.add),
                                   reads=[b_scores[t], b_qkv[2][t], b_o32],
                                   writes=[b_o32], noself=True, acc=True)

            def attn_proj(wi, wname, wd, sub=None):
                for oc in range(4):
                    slot, sb, fam = nextA()
                    load_w(stidx, slot, sb, fam, wd[oc], scr[wname][oc],
                           b_scr[wname])
                    for t in range(NT):
                        pi = next_pm()
                        for kc in range(16):
                            mm(pm[pi][:], fmajT[:, kc, t * 128:(t + 1) * 128],
                               slot[:, kc, :], kc == 0, kc == 15,
                               [b_fmajT[t], sb], b_pm[pi])
                        c0 = wi * NT * D + t * D + oc * 512
                        P.emit("S", lambda eng, pi=pi, c0=c0: eng.copy(
                            qkv[:, c0:c0 + 512], pm[pi][:]),
                            reads=[b_pm[pi]], writes=[b_qkv[wi][t]],
                            hazards=al_w[wi], acc=(oc > 0))
                        if sub is not None:
                            sub(t, oc)

            def attn_fin(t):
                P.emit("S", lambda eng: eng.copy(obf[:], o32[:]),
                       reads=[b_o32], writes=[b_obf])
                P.emit("S", lambda eng: eng.nop(),
                       reads=[b_obf], writes=[b_oT[t]], hazards=al_oT)
                for kq in range(4):
                    pe_tp4(othsu[:, kq * 4:kq * 4 + 4, t * 128:(t + 1) * 128],
                           obf, kq * 4, reads=[b_obf], writes=[b_oT[t]],
                           acc=True)

            attn_proj(0, "wq", wq_d)
            attn_proj(1, "wk", wk_d, sub=attn_scores_sub)
            attn_proj(2, "wv", wv_d)
            for t in range(NT):
                attn_o(t)
                attn_fin(t)

            def ln2_chain(t):
                r0 = row0 + t * 128
                P.emit("V", lambda eng, t=t: eng.bn_aggr(
                    mv[:], stats[:, t, :, :].rearrange("p a b -> p (a b)")),
                    reads=[b_stats[t]], writes=[b_mv])
                P.emit("V", lambda eng: eng.tensor_scalar(
                    lncols[:, 0:1], mv[:, 1:2], EPS, None, ALU.add),
                    reads=[b_mv], writes=[b_lncols])
                P.emit("S", lambda eng: eng.activation(
                    lncols[:, 3:4], lncols[:, 0:1], AF.Ln),
                    reads=[b_lncols], writes=[b_lncols], acc=True)
                P.emit("S", lambda eng, t=t: eng.activation(
                    m2r2[:, t, 0:1], lncols[:, 3:4], AF.Exp, scale=-0.5),
                    reads=[b_lncols], writes=[b_m2r2], acc=(t > 0))
                P.emit("V", lambda eng, t=t: eng.tensor_scalar(
                    m2r2[:, t, 1:2], mv[:, 0:1], m2r2[:, t, 0:1], -1.0,
                    ALU.mult, ALU.mult),
                    reads=[b_mv, b_m2r2], writes=[b_m2r2], acc=True)
                for oc in range(4):
                    sl = oc
                    dma("S", x2st[:, sl, :],
                        x2scr_d[r0:r0 + 128, oc * 512:(oc + 1) * 512],
                        f"x2ld{sl}", reads=[b_x2scr], writes=[b_x2st[sl]])
                    P.emit("S", lambda eng, t=t, oc=oc, sl=sl: eng.activation(
                        lnbf[:, oc * 512:(oc + 1) * 512], x2st[:, sl, :],
                        AF.Identity, bias=m2r2[:, t, 1:2],
                        scale=m2r2[:, t, 0:1]),
                        reads=[b_x2st[sl], b_m2r2], writes=[b_lnbf],
                        acc=(oc > 0))
                P.emit("S", lambda eng: eng.nop(),
                       reads=[b_lnbf], writes=[b_fmajT[t]])
                for kq in range(4):
                    pe_tp4(fmajT[:, kq * 4:kq * 4 + 4, t * 128:(t + 1) * 128],
                           lnbf, kq * 4, reads=[b_lnbf],
                           writes=[b_fmajT[t]], acc=True)

            for oc in range(4):
                slot, sb, fam = nextA()
                load_w(stidx, slot, sb, fam, wo_d[oc], scr["wo"][oc],
                       b_scr["wo"])
                for t in range(NT):
                    r0 = row0 + t * 128
                    pi = next_pm()
                    for kc in range(16):
                        mm(pm[pi][:], othsu[:, kc, t * 128:(t + 1) * 128],
                           slot[:, kc, :], kc == 0, kc == 15,
                           [b_oT[t], sb], b_pm[pi])
                    sl = t
                    dma("S", xst[:, sl, :],
                        x_d[r0:r0 + 128, oc * 512:(oc + 1) * 512],
                        f"xld{sl}", writes=[b_xst[sl]])
                    P.emit("V", lambda eng, pi=pi, sl=sl: eng.tensor_add(
                        x2st[:, sl, :], xst[:, sl, :], pm[pi][:]),
                        reads=[b_xst[sl], b_pm[pi]], writes=[b_x2st[sl]])
                    P.emit("V", lambda eng, t=t, oc=oc, sl=sl: eng.bn_stats(
                        stats[:, t, oc, :], x2st[:, sl, :]),
                        reads=[b_x2st[sl]], writes=[b_stats[t]], acc=(oc > 0))
                    dma("S", x2scr_d[r0:r0 + 128, oc * 512:(oc + 1) * 512],
                        x2st[:, sl, :], rot("sx", 2), reads=[b_x2st[sl]],
                        writes=[b_x2scr], acc=not (oc == 0 and t == 0))
                    if oc == 3:
                        ln2_chain(t)

            for j in range(2):
                for t in range(NT):
                    for kc in range(16):
                        mm(pm[6][:, 0:8], fmajT[:, kc, t * 128:(t + 1) * 128],
                           wg_sb[:, kc, j * 8:(j + 1) * 8], kc == 0, False,
                           [b_fmajT[t], b_const], b_pm[6])
                    mm(pm[6][:, 0:8], selg[0:2, j * 128:(j + 1) * 128],
                       bg2d[0:2, :], False, True, [b_const], b_pm[6])
                    P.emit("S", lambda eng, t=t: eng.copy(
                        g8[:, t, :], pm[6][:, 0:8]),
                        reads=[b_pm[6]], writes=[b_g8[t]])
                    gap = g8[:, t, :]
                    P.emit("V", lambda eng, g=gap: eng.tensor_reduce(
                        rcol[:, 0:1], g, AX.X, ALU.min),
                        reads=[b_g8[t]], writes=[b_rcol])
                    P.emit("V", lambda eng, g=gap: eng.tensor_scalar(
                        rtmp[:, 0, :], g, rcol[:, 0:1], None, ALU.is_equal),
                        reads=[b_g8[t], b_rcol], writes=[b_rtmp])
                    P.emit("V", lambda eng, g=gap: eng.scalar_tensor_tensor(
                        rtmp[:, 1, :], rtmp[:, 0, :], 1e30, g,
                        ALU.mult, ALU.add),
                        reads=[b_g8[t], b_rtmp], writes=[b_rtmp], acc=True)
                    P.emit("V", lambda eng: eng.tensor_reduce(
                        rcol[:, 1:2], rtmp[:, 1, :], AX.X, ALU.min),
                        reads=[b_rtmp], writes=[b_rcol], acc=True)
                    P.emit("V", lambda eng, g=gap: eng.tensor_scalar(
                        rtmp[:, 1, :], g, rcol[:, 1:2], None, ALU.is_gt),
                        reads=[b_g8[t], b_rcol], writes=[b_rtmp], acc=True)
                    P.emit("V", lambda eng, g=gap: eng.reduce_max(
                        rcol[:, 2:3], g, AX.X, negate=True),
                        reads=[b_g8[t]], writes=[b_rcol], acc=True)
                    P.emit("S", lambda eng, g=gap: eng.activation(
                        rtmp[:, 2, :], g, AF.Exp, bias=rcol[:, 2:3]),
                        reads=[b_g8[t], b_rcol], writes=[b_rtmp], acc=True)
                    P.emit("V", lambda eng: eng.tensor_mul(
                        rtmp[:, 0, :], rtmp[:, 2, :], rtmp[:, 1, :]),
                        reads=[b_rtmp], writes=[b_rtmp], acc=True)
                    P.emit("V", lambda eng: eng.tensor_reduce(
                        rcol[:, 3:4], rtmp[:, 0, :], AX.X, ALU.add),
                        reads=[b_rtmp], writes=[b_rcol], acc=True)
                    P.emit("V", lambda eng: eng.reciprocal(
                        rcol[:, 4:5], rcol[:, 3:4]),
                        reads=[b_rcol], writes=[b_rcol], acc=True)
                    P.emit("V", lambda eng, t=t: eng.tensor_scalar(
                        w_r[:, t, :], rtmp[:, 0, :], rcol[:, 4:5], None,
                        ALU.mult),
                        reads=[b_rtmp, b_rcol], writes=[b_w[t]])
                    P.emit("S", lambda eng, t=t: eng.copy(wbf[:], w_r[:, t, :]),
                           reads=[b_w[t]], writes=[b_wbf])
                    P.emit("T", lambda eng: eng.transpose(
                        ptp[0:8, 0:128], wbf[:], ident[:]),
                        reads=[b_wbf, b_const], writes=[b_ptp])
                    P.emit("S", lambda eng, t=t: eng.copy(
                        wT[0:8, t * 128:(t + 1) * 128], ptp[0:8, 0:128]),
                        reads=[b_ptp], writes=[b_wT], acc=(t > 0))

                for ex in range(6):
                    bws = ex % 2
                    mm(pm[6][:, 0:512], sel8[0:8, ex * 128:(ex + 1) * 128],
                       wT[0:8, :], True, True, [b_const, b_wT], b_pm[6])
                    P.emit("S", lambda eng, s=bws: eng.copy(
                        bw_sb[s][:], pm[6][:, 0:512]),
                        reads=[b_pm[6]], writes=[b_bw[bws]])
                    for och in range(2):
                        slot, sb, fam = nextA()
                        load_w(stidx, slot, sb, fam, wup_d[ex, och],
                               scr["wup"][ex, och], b_scr["wup"])
                        for ofc in range(4):
                            pi = next_pm()
                            for kc in range(16):
                                mm(pm[pi][:],
                                   slot[:, kc, ofc * 128:(ofc + 1) * 128],
                                   fmajT[:, kc, :], kc == 0, kc == 15,
                                   [sb] + b_fmajT, b_pm[pi])
                            cc = och * 4 + ofc
                            P.emit("V", lambda eng, pi=pi, cc=cc, ex=ex:
                                   eng.tensor_scalar(
                                       othsu[:, cc, :], pm[pi][:],
                                       bupc[:, ex * 8 + cc:ex * 8 + cc + 1],
                                       None, ALU.add),
                                   reads=[b_pm[pi], b_const],
                                   writes=[b_hF], hazards=al_h,
                                   acc=(och * 4 + ofc > 0))
                    for och in range(2):
                        slot, sb, fam = nextB()
                        load_w(stidx, slot, sb, fam, wsw_d[ex, och],
                               scr["wsw"][ex, och], b_scr["wsw"])
                        for ofc in range(4):
                            pi = next_pm()
                            for kc in range(8):
                                mm(pm[pi][:],
                                   slot[:, kc, ofc * 128:(ofc + 1) * 128],
                                   othsu[:, kc, :], kc == 0, kc == 7,
                                   [b_hF, sb], b_pm[pi])
                            cc = och * 4 + ofc
                            P.emit("S", lambda eng, pi=pi, cc=cc, ex=ex:
                                   eng.activation(
                                       othsu[:, 8 + cc, :], pm[pi][:],
                                       AF.Silu,
                                       bias=bswc[:, ex * 8 + cc:
                                                 ex * 8 + cc + 1]),
                                   reads=[b_pm[pi], b_const],
                                   writes=[b_suF], hazards=al_su,
                                   acc=(och * 4 + ofc > 0))
                    gc0 = ex * 4096
                    for cc in range(8):
                        P.emit("V", lambda eng, cc=cc: eng.tensor_mul(
                            othsu[:, 8 + cc, :], othsu[:, 8 + cc, :],
                            othsu[:, cc, :]),
                            reads=[b_hF, b_suF], writes=[b_suF], acc=True,
                            noself=True)
                        P.emit("V", lambda eng, cc=cc, gc0=gc0, s=bws:
                               eng.tensor_mul(
                                   qkv[:, gc0 + cc * 512:gc0 + (cc + 1) * 512],
                                   othsu[:, 8 + cc, :], bw_sb[s][:]),
                               reads=[b_suF, b_bw[bws]], writes=[b_gf[ex]],
                               hazards=al_gf[ex], acc=(cc > 0), noself=True)

                for c in range(2):
                    for q2 in range(2):
                        slot, sb, fam = nextA()
                        ocu = c * 2 + q2
                        load_w(stidx, slot, sb, fam, wsu_d[ocu],
                               scr["wsu"][ocu], b_scr["wsu"])
                        for ofc in range(4):
                            pi = next_pm()
                            for kc in range(16):
                                mm(pm[pi][:],
                                   slot[:, kc, ofc * 128:(ofc + 1) * 128],
                                   fmajT[:, kc, :], kc == 0, kc == 15,
                                   [sb] + b_fmajT, b_pm[pi])
                            cc = q2 * 4 + ofc
                            P.emit("V", lambda eng, pi=pi, cc=cc, c=c:
                                   eng.tensor_scalar(
                                       shuT[:, cc, :], pm[pi][:],
                                       bsuc[:, c * 8 + cc:c * 8 + cc + 1],
                                       None, ALU.add),
                                   reads=[b_pm[pi], b_const], writes=[b_shuT],
                                   acc=(q2 * 4 + ofc > 0))
                    for och in range(2):
                        slot, sb, fam = nextB()
                        load_w(stidx, slot, sb, fam, wsa_d[och],
                               scr["wsa"][och], b_scr["wsa"])
                        for ofc in range(4):
                            pi = next_pm()
                            for kc in range(8):
                                mm(pm[pi][:],
                                   slot[:, kc, ofc * 128:(ofc + 1) * 128],
                                   shuT[:, kc, :], kc == 0, kc == 7,
                                   [b_shuT, sb], b_pm[pi])
                            cc = och * 4 + ofc
                            P.emit("S", lambda eng, pi=pi, cc=cc:
                                   eng.activation(
                                       othsu[:, cc, :], pm[pi][:], AF.Silu,
                                       bias=bsac[:, cc:cc + 1]),
                                   reads=[b_pm[pi], b_const], writes=[b_hF],
                                   hazards=al_h, acc=(och * 4 + ofc > 0))
                    for cc in range(8):
                        P.emit("V", lambda eng, cc=cc, c=c: eng.tensor_mul(
                            swF[:, c * 8 + cc, :], othsu[:, cc, :],
                            shuT[:, cc, :]),
                            reads=[b_hF, b_shuT], writes=[b_swF],
                            acc=not (c == 0 and cc == 0))

                if j == 1 and stidx + 1 < nst:
                    emit_ln1(stidx + 1)
                for oc in range(4):
                    pis = [0, 1, 2, 3] if oc % 2 == 0 else [4, 5, 6, 0]
                    for t in range(NT):
                        r0 = row0 + t * 128
                        dma("S", x2st[:, t, :],
                            x2scr_d[r0:r0 + 128, oc * 512:(oc + 1) * 512],
                            f"x2ld{t}", reads=[b_x2scr], writes=[b_x2st[t]])
                    for ei in range(6):
                        slot, sb, fam = nextB()
                        load_w(stidx, slot, sb, fam, wdn_d[ei, oc],
                               scr["wdn"][ei, oc], b_scr["wdn"])
                        gv = gf_view(ei)
                        for t in range(NT):
                            for kc in range(8):
                                mm(pm[pis[t]][:],
                                   gv[:, kc, t * 128:(t + 1) * 128],
                                   slot[:, kc, :], ei == 0 and kc == 0, False,
                                   [b_gf[ei], sb], b_pm[pis[t]])
                    slot, sb, fam = nextA()
                    load_w(stidx, slot, sb, fam, wsd_d[oc], scr["wsd"][oc],
                           b_scr["wsd"])
                    for t in range(NT):
                        for kc in range(16):
                            mm(pm[pis[t]][:],
                               swF[:, kc, t * 128:(t + 1) * 128],
                               slot[:, kc, :], False, False,
                               [b_swF, sb], b_pm[pis[t]])
                    for t in range(NT):
                        mm(pm[pis[t]][:], wT[0:6, t * 128:(t + 1) * 128],
                           bdn2d[0:6, oc * 512:(oc + 1) * 512], False, False,
                           [b_wT, b_const], b_pm[pis[t]])
                        mm(pm[pis[t]][:], ones_s[0:1, :],
                           bsd_r[0:1, oc * 512:(oc + 1) * 512], False, True,
                           [b_const], b_pm[pis[t]])
                    for t in range(NT):
                        r0 = row0 + t * 128
                        pi = pis[t]
                        if j == 0:
                            sl = t % 2
                            P.emit("S", lambda eng, t=t, sl=sl: eng.activation(
                                xt1bf[:, sl, :], x2st[:, t, :], AF.Identity,
                                bias=m2r2[:, t, 1:2], scale=m2r2[:, t, 0:1]),
                                reads=[b_x2st[t], b_m2r2],
                                writes=[b_xt1bf[sl]])
                            P.emit("V", lambda eng, pi=pi, sl=sl:
                                   eng.tensor_add(
                                       xt1bf[:, sl, :], xt1bf[:, sl, :],
                                       pm[pi][:]),
                                   reads=[b_pm[pi], b_xt1bf[sl]],
                                   writes=[b_xt1bf[sl]], acc=True)
                            for kq in range(4):
                                dma("Y", fmajT[:, oc * 4 + kq,
                                               t * 128:(t + 1) * 128],
                                    xt1bf[:, sl, kq * 128:(kq + 1) * 128],
                                    rot("tf", 4),
                                    reads=[b_xt1bf[sl]], writes=[b_fmajT[t]],
                                    acc=not (oc == 0 and kq == 0),
                                    transpose=True)
                        else:
                            sl = t % 2
                            P.emit("S", lambda eng, t=t, sl=sl: eng.activation(
                                outst[:, sl, :], x2st[:, t, :], AF.Identity,
                                bias=m2r2[:, t, 1:2], scale=m2r2[:, t, 0:1]),
                                reads=[b_x2st[t], b_m2r2],
                                writes=[b_outst[sl]])
                            P.emit("V", lambda eng, pi=pi, sl=sl:
                                   eng.tensor_add(
                                       outst[:, sl, :], outst[:, sl, :],
                                       pm[pi][:]),
                                   reads=[b_pm[pi], b_outst[sl]],
                                   writes=[b_outst[sl]], acc=True)
                            P.emit("V", lambda eng, t=t, sl=sl: eng.tensor_add(
                                outst[:, sl, :], outst[:, sl, :],
                                x2st[:, t, :]),
                                reads=[b_x2st[t], b_outst[sl]],
                                writes=[b_outst[sl]], acc=True)
                            dma("S", y_d[r0:r0 + 128, oc * 512:(oc + 1) * 512],
                                outst[:, sl, :], rot("sy", 2),
                                reads=[b_outst[sl]])

        emit_ln1(0)
        for stidx in range(nst):
            emit_supertile(stidx)

        P.barrier("S", "sy0", P.cnt["sy0"])
        P.barrier("S", "sy1", P.cnt["sy1"])
        P.barrier("Y", "sy0", P.cnt["sy0"])
        P.barrier("Y", "sy1", P.cnt["sy1"])

        with nc.Block() as block:
            @block.tensor
            def _(eng):
                P.replay("T", eng)

            @block.vector
            def _(eng):
                P.replay("V", eng)

            @block.scalar
            def _(eng):
                P.replay("S", eng)

            @block.sync
            def _(eng):
                P.replay("Y", eng)

            @block.gpsimd
            def _(eng):
                P.replay("G", eng)

    return nc



def _chunkK(w):
    K, F = w.shape
    return np.ascontiguousarray(
        w.reshape(K // 128, 128, F // 512, 512).transpose(2, 1, 0, 3))


def _prep_inputs(inp):
    f32 = lambda a: np.asarray(a, dtype=np.float32)
    wq = f32(inp["L0_attn_Wq"]); wk = f32(inp["L0_attn_Wk"])
    wv = f32(inp["L0_attn_Wv"]); wo = f32(inp["L0_attn_Wo"])
    wg = f32(inp["L0_ffn_Wg"]); bg = f32(inp["L0_ffn_bg"])
    wup = f32(inp["L0_ffn_Wup"]); bup = f32(inp["L0_ffn_bup"])
    wsw = f32(inp["L0_ffn_Wsw"]); bsw = f32(inp["L0_ffn_bsw"])
    wdn = f32(inp["L0_ffn_Wdn"]); bdn = f32(inp["L0_ffn_bdn"])
    wsu = f32(inp["L0_ffn_Wsu"]); bsu = f32(inp["L0_ffn_bsu"])
    wsa = f32(inp["L0_ffn_Wsa"]); bsa = f32(inp["L0_ffn_bsa"])
    wsd = f32(inp["L0_ffn_Wsd"]); bsd = f32(inp["L0_ffn_bsd"])

    bf = ml_dtypes.bfloat16
    sel8 = np.zeros((8, 1024), dtype=bf)
    for ei in range(8):
        sel8[ei, ei * 128:(ei + 1) * 128] = 1
    selg = np.zeros((2, 256), dtype=bf)
    for ji in range(2):
        selg[ji, ji * 128:(ji + 1) * 128] = 1

    shared = {
        "wq_c": _chunkK(wq), "wk_c": _chunkK(wk), "wv_c": _chunkK(wv),
        "wo_c": _chunkK(wo),
        "wup_c": np.stack([_chunkK(wup[i]) for i in range(6)]),
        "wsw_c": np.stack([_chunkK(wsw[i]) for i in range(6)]),
        "wdn_c": np.stack([_chunkK(wdn[i]) for i in range(6)]),
        "wsu_c": _chunkK(wsu), "wsa_c": _chunkK(wsa), "wsd_c": _chunkK(wsd),
        "wg_c": np.ascontiguousarray(
            wg.reshape(2, 16, 128, 8).transpose(2, 1, 0, 3).reshape(128, 16, 16)),
        "bupc": np.ascontiguousarray(
            bup.reshape(6, 8, 128).transpose(2, 0, 1).reshape(128, 48)),
        "bswc": np.ascontiguousarray(
            bsw.reshape(6, 8, 128).transpose(2, 0, 1).reshape(128, 48)),
        "bsac": np.ascontiguousarray(bsa.reshape(8, 128).T),
        "bsuc": np.ascontiguousarray(bsu.reshape(16, 128).T),
        "bdn2d": np.ascontiguousarray(bdn),
        "bsd_r": np.ascontiguousarray(bsd[None, :]),
        "bg2d": np.ascontiguousarray(bg),
        "ones_c": np.ones((1, 128), dtype=bf),
        "sel8_c": sel8, "selg_c": selg,
        "ident_c": np.eye(128, dtype=bf),
    }
    return shared


def kernel(**inputs):
    x = np.asarray(inputs["L0_attn_x"], dtype=np.float32)
    B, N, _ = x.shape
    xf = np.ascontiguousarray(x.reshape(B * N, D))
    shared = _prep_inputs(inputs)
    nc = build_program(TPC)
    in_maps = [dict(shared, x=np.ascontiguousarray(xf[c * TPC:(c + 1) * TPC]))
               for c in range(N_CORES)]
    res = run_bass_kernel_spmd(nc, in_maps, core_ids=list(range(N_CORES)))
    y = np.concatenate([r["y"] for r in res.results], axis=0)
    return y.reshape(B, N, D)
